# version 02
# baseline (speedup 1.0000x reference)
import numpy as np

import concourse.bass as bass
import concourse.mybir as mybir
from concourse.bass_utils import run_bass_kernel_spmd

F32 = mybir.dt.float32
AF = mybir.ActivationFunctionType
ALU = mybir.AluOpType
AX = mybir.AxisListType

NCORE = 8
T, D, INTER = 8192, 2048, 8192
TO = T // NCORE
H, HKV, DV = 16, 4, 64
REMOTE, MBLK, LW = 7168, 64, 1024
NBLK = REMOTE // MBLK
NKEY = NBLK + LW
NKEYP = 1152
NEG, NEGPAD, POS = -1.0e9, -2.0e9, 1.0e30
HG = 256
NHG = T // HG
NOG = TO // HG
NQT = TO // 128
EPS = 1e-5

DEBUG = False
DEBUG2 = False
PE1_BARRIER = False
PHASES = None
ENGINES = ("te", "ve", "se", "ge", "sy")
SHUF16 = [(i + 16) % 32 for i in range(32)]


def _had(n):
    h = np.array([[1.0]])
    while h.shape[0] < n:
        h = np.block([[h, h], [h, -h]])
    return h


class B:
    def __init__(self, nc):
        self.nc = nc
        self.ops = {e: [] for e in ENGINES}
        self.cnt = {}
        self.sems = {}
        self.enabled = True


    def phase(self, name):
        self.enabled = (PHASES is None) or (name in PHASES)

    def sem(self, name):
        if name not in self.sems:
            self.sems[name] = self.nc.alloc_semaphore(name)
            self.cnt[name] = 0
        return self.sems[name]

    def W(self, eng, sem, val):
        if not self.enabled:
            return
        if not val or val <= 0:
            return
        self.sem(sem)
        self.ops[eng].append(("w", sem, val))

    def _emit(self, eng, meth, args, kw, inc):
        if not self.enabled:
            return self.cnt.get(inc[0], 0) if inc is not None else None
        if inc is not None:
            self.sem(inc[0])
            self.cnt[inc[0]] += inc[1]
        self.ops[eng].append(("e", meth, args, kw, inc))
        return self.cnt[inc[0]] if inc is not None else None

    def ve(self, meth, *a, **kw):
        self.W("ve", "pve", self.cnt.get("pve", 0))
        return self._emit("ve", meth, a, kw, ("pve", 1))

    def se(self, meth, *a, **kw):
        self.W("se", "pse", self.cnt.get("pse", 0))
        return self._emit("se", meth, a, kw, ("pse", 1))

    def ge(self, meth, *a, **kw):
        self.W("ge", "pge", self.cnt.get("pge", 0))
        return self._emit("ge", meth, a, kw, ("pge", 1))

    def mm(self, out, lhsT, rhs, start=True, stop=True, inc=True, **kw):
        return self._emit("te", "matmul", (out, lhsT, rhs),
                          dict(start=start, stop=stop, **kw),
                          ("pte", 1) if inc else None)

    def tr(self, out, in_, ident):
        return self._emit("te", "matmul", (out, in_, ident),
                          dict(is_transpose=True), ("pte", 1))

    def dma(self, q, out, in_, sem=None):
        return self._emit(q, "dma_start", (), dict(out=out, in_=in_),
                          (sem or ("d" + q), 16))

    def barrier(self):
        snap = [(s, v) for s, v in self.cnt.items() if v > 0]
        for e in ENGINES:
            for s, v in snap:
                self.W(e, s, v)

    def finish(self):
        with self.nc.Block() as block:
            regs = {"sy": block.sync, "te": block.tensor, "ve": block.vector,
                    "se": block.scalar, "ge": block.gpsimd}
            for name in ENGINES:
                lst = self.ops[name]

                def body(E, lst=lst):
                    for op in lst:
                        if op[0] == "w":
                            E.wait_ge(self.sems[op[1]], op[2])
                        else:
                            _, meth, args, kw, inc = op
                            i = getattr(E, meth)(*args, **kw)
                            if inc is not None:
                                i.then_inc(self.sems[inc[0]], inc[1])
                regs[name](body)


def build_program():
    nc = bass.Bass("TRN2", target_bir_lowering=False, debug=False,
                   detect_race_conditions=False)
    b = B(nc)
    for s_ in ("pte", "pve", "pse", "pge", "dsy", "dge", "dte", "dve", "dse"):
        b.sem(s_)

    din = lambda n, s: nc.dram_tensor(n, s, F32, kind="ExternalInput").ap()
    xf = din("xf", [T, D])
    xo = din("xo", [TO, D])
    wq_d = din("wq", [16, 128, 1024])
    wkv_d = din("wkv", [128, 8192])
    wup_d = din("wup", [64, 128, 2048])
    wgate_d = din("wgate", [64, 128, 2048])
    wdown_d = din("wdown", [4, 64, 128, 512])
    wo_d = din("wo", [16, 128, 1024])
    suu_d = din("su_ug", [16, 128])
    sud_d = din("su_dp", [64, 128])
    sug_d = din("wsc_ug", [128, 128])
    sdp_d = din("wsc_dp", [16, 128])
    ls_d = din("ls", [1, 16])
    mask_d = din("mask", [NQT, 128, NKEYP])
    ropeq_d = din("ropeq", [2, 128, TO])
    ropek_d = din("ropek", [NHG, 2, 128, HG])
    out_d = nc.dram_tensor("out", [TO, D], F32, kind="ExternalOutput").ap()

    hc_d = nc.inline_tensor((_had(128) / 64.0).astype(np.float32),
                            name="hado64").ap()
    id_d = nc.inline_tensor(np.eye(128, dtype=np.float32), name="identc").ap()
    ones_d = nc.inline_tensor(np.ones((1, 128), dtype=np.float32),
                              name="onesr").ap()
    cv_d = nc.inline_tensor(np.array([[32.0, 2048.0, 8192.0, 1.0]],
                                     dtype=np.float32), name="cvals").ap()
    eps_d = nc.inline_tensor(np.full((1, 1), EPS, dtype=np.float32),
                             name="epsc").ap()

    xn_d = nc.dram_tensor("xn_s", [16, 128, TO], F32, kind="Internal").ap()
    y_d = nc.dram_tensor("y_s", [64, 128, TO], F32, kind="Internal").ap()
    x2_d = nc.dram_tensor("x2_s", [64, 128, TO], F32, kind="Internal").ap()

    perm = {}
    for name, shape in [
            ("hc", [128, 128]), ("idn", [128, 128]), ("ones", [1, 128]),
            ("cv", [1, 4]), ("rcv", [1, 4]), ("rcvb", [128, 4]),
            ("suu", [128, 16]), ("sud", [128, 64]),
            ("sug", [128, 128]), ("sdp", [128, 16]),
            ("lse", [1, 16]), ("qsc", [128, 16]), ("smv", [128, 8]),
            ("epsb", [128, 1]), ("eps1", [1, 1]),
            ("hsu_ug", [128, 2048]),
            ("kp", [128, 2 * NKEYP]), ("kps", [128, 2 * NKEYP]),
            ("vp", [128, 2 * NKEYP]),
            ("vpt", [128, 9 * 256]),
            ("attnv", [128, 8 * TO]),
    ]:
        perm[name] = nc.alloc_sbuf_tensor(name, shape, F32)
    z0 = nc.alloc_sbuf_tensor("z0", [128, 16384], F32)
    z1 = nc.alloc_sbuf_tensor("z1", [128, 16384], F32)
    Z0 = lambda off, n, p=128: z0[0:p, off:off + n]
    Z1 = lambda off, n, p=128: z1[0:p, off:off + n]
    P = {k: v[:] for k, v in perm.items()}

    pb = [nc.alloc_psum_tensor(f"pb{i}", [128, 512], F32) for i in range(8)]

    dbg2 = {}
    if DEBUG2:
        for name, shape in [("e_attnv", [128, 8 * TO]), ("e_y", [64, 128, TO]),
                            ("e_x2", [64, 128, TO]), ("e_xn", [16, 128, TO]),
                            ("e_x1", [128, 16384])]:
            dbg2[name] = nc.dram_tensor(name, shape, F32,
                                        kind="ExternalOutput").ap()
    dbg = {}
    if DEBUG:
        for name, shape in [("d_kp", [128, 2 * NKEYP]), ("d_vp", [128, 2 * NKEYP]),
                            ("d_qhat", [128, 8192]), ("d_attnv", [128, 8 * TO]),
                            ("d_x1", [128, 16384]), ("d_acc", [128, 16384]),
                            ("d_xn", [16, 128, TO]), ("d_vpt", [128, 9 * 256]),
                            ("d_y", [64, 128, TO]), ("d_x2", [64, 128, TO])]:
            dbg[name] = nc.dram_tensor(name, shape, F32,
                                       kind="ExternalOutput").ap()

    b.phase("P0")
    b.dma("sy", P["hc"], hc_d)
    b.dma("sy", P["idn"], id_d)
    b.dma("sy", P["ones"], ones_d)
    b.dma("sy", P["cv"], cv_d)
    b.dma("sy", P["eps1"], eps_d)
    b.dma("sy", P["lse"], ls_d)
    st_suu = Z1(4608, 128, 16)
    st_sud = Z1(4736, 128, 64)
    st_sug = Z1(4864, 128, 128)
    st_sdp = Z1(4992, 128, 16)
    b.dma("sy", st_suu, suu_d)
    b.dma("sy", st_sud, sud_d)
    b.dma("sy", st_sug, sug_d)
    b.dma("sy", st_sdp, sdp_d)
    b.dma("sy", Z0(8192, 8192), wkv_d)
    v_c = b.cnt["dsy"]

    b.W("se", "dsy", v_c)
    b.se("activation", P["rcv"], P["cv"], AF.Sqrt)
    vse = b.se("activation", P["lse"], P["lse"], AF.Exp)
    b.W("ve", "pse", vse)
    b.ve("reciprocal", P["rcv"], P["rcv"])
    vve = b.ve("tensor_scalar_mul", P["lse"], P["lse"], P["rcv"][0:1, 0:1])
    b.W("te", "pve", vve)
    b.W("te", "dsy", v_c)
    b.mm(pb[0][:, 0:4], P["ones"], P["rcv"], inc=False)
    b.mm(pb[0][:, 8:9], P["ones"], P["eps1"], inc=False)
    b.mm(pb[0][:, 16:32], P["ones"], P["lse"], inc=False)
    b.tr(pb[1][:, 0:16], st_suu, P["idn"][0:16, 0:16])
    b.tr(pb[1][:, 16:80], st_sud, P["idn"][0:64, 0:64])
    b.tr(pb[2][:, 0:128], st_sug, P["idn"])
    vte = b.tr(pb[1][:, 80:96], st_sdp, P["idn"][0:16, 0:16])
    b.W("ve", "pte", vte)
    b.ve("tensor_copy", P["rcvb"], pb[0][:, 0:4])
    b.ve("tensor_copy", P["epsb"], pb[0][:, 8:9])
    b.ve("tensor_copy", P["qsc"], pb[0][:, 16:32])
    b.ve("tensor_copy", P["suu"], pb[1][:, 0:16])
    b.ve("tensor_copy", P["sud"], pb[1][:, 16:80])
    b.ve("tensor_copy", P["sdp"], pb[1][:, 80:96])
    b.ve("tensor_copy", P["sug"], pb[2][:, 0:128])
    for m in range(8):
        b.ve("tensor_copy", P["smv"][0:64, m:m + 1],
             P["qsc"][0:64, 2 * m:2 * m + 1])
        b.ve("tensor_copy", P["smv"][64:128, m:m + 1],
             P["qsc"][64:128, 2 * m + 1:2 * m + 2])
    b.ve("tensor_scalar_mul", P["sug"], P["sug"], 64.0)
    b.ve("tensor_scalar_mul", P["sug"], P["sug"], P["rcvb"][:, 1:2])
    b.ve("tensor_scalar_mul", P["sdp"], P["sdp"], 64.0)
    b.ve("tensor_scalar_mul", P["sdp"], P["sdp"], P["rcvb"][:, 2:3])
    for a in range(16):
        b.ve("tensor_scalar_mul", P["hsu_ug"][:, a * 128:(a + 1) * 128],
             P["hc"], P["suu"][:, a:a + 1])
    b.ve("memset", P["kp"], 0.0)
    b.ve("memset", P["vp"], 0.0)
    b.barrier()

    b.phase("PA")
    xtm = lambda tt: Z0(tt * 2048, 2048)
    xng = lambda c: Z0(4096 + c * 256, 256)
    wkv_s = Z0(8192, 8192)
    sqscr = Z1(0, 2048)
    stc = lambda tt, j: Z1(2048 + tt * 8 + j, 1)
    khg = lambda kt: Z1(2112 + kt * 256, 256)
    khg2 = lambda kt: Z1(2624 + kt * 256, 256)
    vhg = lambda kt: Z1(3136 + kt * 256, 256)
    rkA = Z1(3648, 256)
    rkB = Z1(3904, 256)
    pool_a = Z1(4416, 128)
    pool_b = Z1(4544, 64)
    ropet = Z1(5120, 256)

    def stats_norm(xt_ap, tt):
        s1, s2 = stc(tt, 0), stc(tt, 1)
        m, e2, msq, var, r = (stc(tt, 2), stc(tt, 3), stc(tt, 4), stc(tt, 5),
                              stc(tt, 6))
        vs = b.se("activation", sqscr, xt_ap, AF.Square, accum_out=s2)
        b.ve("reduce_sum", s1, xt_ap, axis=AX.X)
        b.ve("tensor_scalar_mul", m, s1, 1.0 / 2048.0)
        b.W("ve", "pse", vs)
        b.ve("tensor_scalar_mul", e2, s2, 1.0 / 2048.0)
        b.ve("tensor_scalar_mul", msq, m, m)
        vv = b.ve("tensor_tensor", var, e2, msq, op=ALU.subtract)
        b.W("se", "pve", vv)
        b.se("activation", r, var, AF.Ln, bias=P["epsb"])
        vs2 = b.se("activation", r, r, AF.Exp, scale=-0.5)
        b.W("ve", "pse", vs2)
        return b.ve("tensor_scalar", xt_ap, xt_ap, m, r,
                    op0=ALU.subtract, op1=ALU.mult)

    def rope(dst, src, A_ap, B_ap, tmp, width):
        b.ve("stream_shuffle", tmp[:, 0:width], src, SHUF16)
        b.ve("tensor_tensor", dst, src, A_ap[:, 0:width], op=ALU.mult)
        b.ve("tensor_tensor", tmp[:, 0:width], tmp[:, 0:width], B_ap[:, 0:width],
             op=ALU.mult)
        return b.ve("tensor_tensor", dst, dst, tmp[:, 0:width], op=ALU.add)

    def pool64(src, dst_ap):
        seq = [(src, 256, pool_a), (pool_a, 128, pool_b), (pool_b, 64, pool_a),
               (pool_a, 32, pool_b), (pool_b, 16, pool_a), (pool_a, 8, pool_b)]
        for s_ap, w, d_ap in seq:
            v = s_ap[:, 0:w].rearrange("p (b w) -> p b w", b=4)
            hw = w // 8
            b.ve("tensor_tensor",
                 d_ap[:, 0:w // 2].rearrange("p (b w) -> p b w", b=4),
                 v[:, :, 0:hw], v[:, :, hw:2 * hw], op=ALU.add)
        return b.ve("tensor_scalar_mul", dst_ap, pool_b[:, 0:4], 1.0 / 64.0)

    ev_xtm_free = 0
    ev_rk_free = 0
    ev_xng = {}
    ev_tbank = {}
    ev_kvbank = 0
    for hg in range(NHG):
        b.W("sy", "pve", ev_xtm_free)
        for tt in range(2):
            b.dma("sy", xtm(tt),
                  xf[hg * HG + tt * 128: hg * HG + (tt + 1) * 128, :],
                  sem="s_xa")
        v_x = b.cnt["s_xa"]
        b.W("sy", "pve", ev_rk_free)
        b.dma("sy", rkA, ropek_d[hg, 0], sem="s_rk")
        b.dma("sy", rkB, ropek_d[hg, 1], sem="s_rk")
        v_rk = b.cnt["s_rk"]

        b.W("se", "s_xa", v_x)
        b.W("ve", "s_xa", v_x)
        norm_ev = [stats_norm(xtm(tt), tt) for tt in range(2)]
        b.W("te", "pve", norm_ev[1])
        for c in range(16):
            bank = c % 2
            b.W("te", "pve", ev_tbank.get(bank, 0))
            for tt in range(2):
                b.tr(pb[bank][:, tt * 128:(tt + 1) * 128],
                     xtm(tt)[:, c * 128:(c + 1) * 128], P["idn"])
            vt = b.cnt["pte"]
            b.W("ve", "pte", vt)
            ev_xng[c] = b.ve("tensor_copy", xng(c), pb[bank][:, 0:256])
            ev_tbank[bank] = ev_xng[c]
        ev_xtm_free = ev_xng[15]

        b.W("te", "pve", ev_xng[15])
        b.W("te", "pve", ev_kvbank)
        for c in range(16):
            for w in range(4):
                b.mm(pb[4 + w][:, 0:256],
                     wkv_s[:, (c * 4 + w) * 128:(c * 4 + w + 1) * 128],
                     xng(c), start=(c == 0), stop=(c == 15),
                     inc=(c == 15 and w == 3))
        v_kv = b.cnt["pte"]
        b.W("ve", "pte", v_kv)
        b.W("ve", "s_rk", v_rk)
        for kt in range(2):
            b.ve("tensor_copy", khg(kt), pb[4 + kt][:, 0:256])
            b.ve("tensor_copy", vhg(kt), pb[6 + kt][:, 0:256])
        for kt in range(2):
            rope(khg2(kt), khg(kt), rkA, rkB, ropet, 256)
        ev_rk_free = b.cnt["pve"]
        if hg < REMOTE // HG:
            for kt in range(2):
                pool64(khg2(kt),
                       P["kp"][:, kt * NKEYP + hg * 4: kt * NKEYP + hg * 4 + 4])
                pool64(vhg(kt),
                       P["vp"][:, kt * NKEYP + hg * 4: kt * NKEYP + hg * 4 + 4])
        else:
            off = NBLK + (hg - REMOTE // HG) * HG
            for kt in range(2):
                b.ve("tensor_copy",
                     P["kp"][:, kt * NKEYP + off: kt * NKEYP + off + HG],
                     khg2(kt))
                b.ve("tensor_copy",
                     P["vp"][:, kt * NKEYP + off: kt * NKEYP + off + HG],
                     vhg(kt))
        ev_kvbank = b.cnt["pve"]

    if DEBUG:
        b.barrier()
        b.dma("ge", dbg["d_kp"], P["kp"])
        b.dma("ge", dbg["d_vp"], P["vp"])

    b.phase("PA2")
    b.barrier()
    ev_xtm_free = 0
    ev_tbank = {}
    ev_xnd = {}
    for og in range(NOG):
        b.W("sy", "pve", ev_xtm_free)
        for tt in range(2):
            b.dma("sy", xtm(tt),
                  xo[og * HG + tt * 128: og * HG + (tt + 1) * 128, :],
                  sem="s_xa")
        v_x = b.cnt["s_xa"]
        b.W("se", "s_xa", v_x)
        b.W("ve", "s_xa", v_x)
        norm_ev = [stats_norm(xtm(tt), tt) for tt in range(2)]
        b.W("te", "pve", norm_ev[1])
        b.W("te", "s_xw", 16 * 16 * og)
        for c in range(16):
            bank = c % 2
            b.W("te", "pve", ev_tbank.get(bank, 0))
            for tt in range(2):
                b.tr(pb[bank][:, tt * 128:(tt + 1) * 128],
                     xtm(tt)[:, c * 128:(c + 1) * 128], P["idn"])
            vt = b.cnt["pte"]
            b.W("ve", "pte", vt)
            vv = b.ve("tensor_copy", xng(c), pb[bank][:, 0:256])
            ev_tbank[bank] = vv
            b.W("sy", "pve", vv)
            b.dma("sy", xn_d[c, :, og * HG:(og + 1) * HG], xng(c), sem="s_xw")
        ev_xtm_free = ev_tbank[1]
    if DEBUG:
        b.barrier()
        b.dma("ge", dbg["d_xn"], xn_d)

    b.phase("VPT")
    b.barrier()
    b.ve("tensor_copy", P["kps"][0:64, :], P["kp"][64:128, :])
    b.ve("tensor_copy", P["kps"][64:128, :], P["kp"][0:64, :])
    ev_vb = {}
    for kc in range(9):
        for kt in range(2):
            bank = (kc * 2 + kt) % 2
            b.W("te", "pve", ev_vb.get(bank, 0))
            vt = b.tr(pb[bank][:, 0:128],
                      P["vp"][:, kt * NKEYP + kc * 128: kt * NKEYP + (kc + 1) * 128],
                      P["idn"])
            b.W("ve", "pte", vt)
            ev_vb[bank] = b.ve("tensor_copy",
                               P["vpt"][:, kc * 256 + kt * 128:
                                        kc * 256 + (kt + 1) * 128],
                               pb[bank][:, 0:128])
    if DEBUG:
        b.barrier()
        b.dma("ge", dbg["d_vpt"], P["vpt"])

    b.phase("PB")
    b.barrier()
    qhat = lambda m: Z0(m * 1024, 1024)
    wqb = lambda s: Z0(8192 + s * 1024, 1024)
    xnc = lambda s: Z0(12288 + s * 512, 512)
    rqA = Z1(0, 1024)
    rqB = Z1(1024, 1024)
    ropeqt = Z1(3072, 1024)
    b.dma("sy", rqA, ropeq_d[0], sem="s_rq")
    b.dma("sy", rqB, ropeq_d[1], sem="s_rq")
    v_rq = b.cnt["s_rq"]
    ev_slot = {}
    ev_qbank = 0
    for half in range(2):
        for c in range(16):
            s = c % 4
            b.W("ge", "pte", ev_slot.get(s, 0))
            b.dma("ge", wqb(s), wq_d[c], sem=f"s_w{s}")
            vw = b.cnt[f"s_w{s}"]
            b.W("sy", "pte", ev_slot.get(s, 0))
            b.dma("sy", xnc(s), xn_d[c, :, half * 512:(half + 1) * 512],
                  sem=f"s_q{s}")
            vx = b.cnt[f"s_q{s}"]
            b.W("te", f"s_w{s}", vw)
            b.W("te", f"s_q{s}", vx)
            if c == 0:
                b.W("te", "pve", ev_qbank)
            for m in range(8):
                b.mm(pb[m][:, 0:512], wqb(s)[:, m * 128:(m + 1) * 128], xnc(s),
                     start=(c == 0), stop=(c == 15), inc=(m == 7))
            ev_slot[s] = b.cnt["pte"]
        v_q = b.cnt["pte"]
        b.W("ve", "pte", v_q)
        for m in range(8):
            b.ve("tensor_scalar_mul",
                 qhat(m)[:, half * 512:(half + 1) * 512], pb[m][:, 0:512],
                 P["smv"][:, m:m + 1])
        ev_qbank = b.cnt["pve"]
    b.W("ve", "s_rq", v_rq)
    for m in range(8):
        rope(qhat(m), qhat(m), rqA, rqB, ropeqt, 1024)
    if DEBUG:
        b.barrier()
        b.dma("ge", dbg["d_qhat"], Z0(0, 8192))

    b.phase("PC")
    b.barrier()
    maskb = lambda s: Z0(8192 + s * NKEYP, NKEYP)
    probs = Z0(10496, NKEYP)
    probsn = Z0(11648, NKEYP)
    probst = Z0(12800, NKEYP)
    red = lambda j: Z0(13952 + j, 1)
    ev_mask = {}
    ev_sc = 0
    ev_pt = 0
    ev_av = 0
    ev_probsn_rd = 0
    ev_probst_rd = 0
    for qt in range(NQT):
        ms = qt % 2
        b.W("sy", "pve", ev_mask.get(ms, 0))
        b.dma("sy", maskb(ms), mask_d[qt], sem=f"s_m{ms}")
        v_m = b.cnt[f"s_m{ms}"]
        for h in range(H):
            g = h // 4
            hh = h // 2
            lhs = qhat(hh)[(h % 2) * 64:(h % 2) * 64 + 64,
                           qt * 128:(qt + 1) * 128]
            kp0 = (g // 2) * NKEYP
            qb = (h % 2) * 64
            ksrc = P["kp"] if (h % 2) == (g % 2) else P["kps"]
            krows = ksrc[qb:qb + 64, :]
            b.W("te", "pve", ev_sc)
            b.mm(pb[0][:, 0:512], lhs, krows[:, kp0:kp0 + 512], inc=False)
            b.mm(pb[1][:, 0:512], lhs, krows[:, kp0 + 512:kp0 + 1024],
                 inc=False)
            v_s = b.mm(pb[2][:, 0:128], lhs, krows[:, kp0 + 1024:kp0 + 1152])
            b.W("ve", "pte", v_s)
            b.W("ve", f"s_m{ms}", v_m)
            b.ve("tensor_tensor", probs[:, 0:512], pb[0][:, 0:512],
                 maskb(ms)[:, 0:512], op=ALU.min)
            b.ve("tensor_tensor", probs[:, 512:1024], pb[1][:, 0:512],
                 maskb(ms)[:, 512:1024], op=ALU.min)
            ev_sc = b.ve("tensor_tensor", probs[:, 1024:1152], pb[2][:, 0:128],
                         maskb(ms)[:, 1024:1152], op=ALU.min)
            vred = b.ve("reduce_max", red(0), probs, axis=AX.X, negate=True)
            b.W("se", "pve", vred)
            vexp = b.se("activation", probs, probs, AF.Exp, bias=red(0),
                        accum_out=red(1))
            b.W("ve", "pse", vexp)
            b.ve("reciprocal", red(2), red(1))
            b.W("ve", "pte", ev_probsn_rd)
            vpn = b.ve("tensor_scalar_mul", probsn, probs, red(2))
            b.W("te", "pve", vpn)
            b.W("te", "pve", ev_pt)
            for kc in range(9):
                b.tr(pb[3 + kc // 4][:, (kc % 4) * 128:(kc % 4) * 128 + 128],
                     probsn[:, kc * 128:(kc + 1) * 128], P["idn"])
            ev_probsn_rd = b.cnt["pte"]
            v_t = b.cnt["pte"]
            b.W("ve", "pte", v_t)
            b.W("ve", "pte", ev_probst_rd)
            b.ve("tensor_copy", probst[:, 0:512], pb[3][:, 0:512])
            b.ve("tensor_copy", probst[:, 512:1024], pb[4][:, 0:512])
            ev_pt = b.ve("tensor_copy", probst[:, 1024:1152], pb[5][:, 0:128])
            b.W("te", "pve", ev_pt)
            b.W("te", "pve", ev_av)
            for kc in range(9):
                b.mm(pb[6][0:64, 0:128],
                     P["vpt"][:, kc * 256 + g * 64: kc * 256 + g * 64 + 64],
                     probst[:, kc * 128:(kc + 1) * 128],
                     start=(kc == 0), stop=(kc == 8), inc=(kc == 8))
            v_av = b.cnt["pte"]
            ev_probst_rd = v_av
            b.W("ve", "pte", v_av)
            ev_av = b.ve("tensor_copy",
                         P["attnv"][(h % 2) * 64:(h % 2) * 64 + 64,
                                    hh * TO + qt * 128: hh * TO + (qt + 1) * 128],
                         pb[6][0:64, 0:128])
        ev_mask[ms] = ev_sc
    if DEBUG:
        b.barrier()
        b.dma("ge", dbg["d_attnv"], P["attnv"])

    b.phase("PE1")
    b.barrier()
    x1 = lambda a: Z0(a * 1024, 1024)
    zba = lambda a: Z1(a * 256, 256)
    zbb = lambda a: Z1(4096 + a * 256, 256)
    xnc1 = lambda s: Z1(8192 + s * 256, 256)
    ev_x1bank = {}
    ev_xnc1 = {}
    for tc in range(4):
        if PE1_BARRIER:
            b.barrier()
        for a in range(16):
            s = a % 4
            b.W("sy", "pte", ev_xnc1.get(s, 0))
            b.dma("sy", xnc1(s), xn_d[a, :, tc * 256:(tc + 1) * 256],
                  sem=f"s_e{s}")
            vx = b.cnt[f"s_e{s}"]
            b.W("te", f"s_e{s}", vx)
            bank = a % 4
            b.W("te", "pve", ev_x1bank.get(bank, 0))
            v_m = b.mm(pb[bank][:, 0:256],
                       P["hsu_ug"][:, a * 128:(a + 1) * 128], xnc1(s))
            ev_xnc1[s] = v_m
            b.W("ve", "pte", v_m)
            ev_x1bank[bank] = b.ve("tensor_copy", zba(a), pb[bank][:, 0:256])
        for stage, hbit in enumerate([1, 2, 4, 8]):
            src = zba if stage % 2 == 0 else zbb
            for i in range(16):
                jlo, jhi = i & ~hbit, i | hbit
                op = ALU.add if (i & hbit) == 0 else ALU.subtract
                if stage < 3:
                    dst = (zbb if stage % 2 == 0 else zba)(i)
                else:
                    dst = x1(i)[:, tc * 256:(tc + 1) * 256]
                b.ve("tensor_tensor", dst, src(jlo), src(jhi), op=op)
    if DEBUG:
        b.barrier()
        b.dma("ge", dbg["d_x1"], Z0(0, 16384))
    if DEBUG2:
        b.barrier()
        b.dma("ge", dbg2["e_x1"], Z0(0, 16384))

    b.phase("P1")
    b.barrier()
    wub = lambda s: Z1(s * 2048, 2048)
    wgb = lambda s: Z1(4096 + s * 2048, 2048)
    hsud = lambda s: Z1(8192 + s * 128, 128)
    gsil = Z1(8448, 1024)
    gtile = Z1(9472, 1024)
    ytile = lambda s: Z1(10496 + s * 1024, 1024)
    ev_w = {}
    ev_hsu = {}
    ev_ybank = 0
    ev_ysb = {}
    ev_upbank = 0
    ev_gdone = 0
    for a in range(64):
        s = a % 2
        b.W("sy", "pte", ev_w.get(s, 0))
        b.dma("sy", wub(s), wup_d[a], sem=f"s_u{s}")
        b.dma("sy", wgb(s), wgate_d[a], sem=f"s_u{s}")
        vw = b.cnt[f"s_u{s}"]
        b.W("ve", "pte", ev_hsu.get(s, 0))
        vh = b.ve("tensor_scalar_mul", hsud(s), P["hc"], P["sud"][:, a:a + 1])
        b.W("te", f"s_u{s}", vw)
        b.W("te", "pve", vh)
        b.W("te", "pse", ev_upbank)
        b.W("te", "pve", ev_gdone)
        for c in range(16):
            for half in range(2):
                b.mm(pb[half][:, 0:512], wub(s)[:, c * 128:(c + 1) * 128],
                     x1(c)[:, half * 512:(half + 1) * 512],
                     start=(c == 0), stop=(c == 15), inc=False)
                b.mm(pb[2 + half][:, 0:512], wgb(s)[:, c * 128:(c + 1) * 128],
                     x1(c)[:, half * 512:(half + 1) * 512],
                     start=(c == 0), stop=(c == 15),
                     inc=(c == 15 and half == 1))
        ev_w[s] = b.cnt["pte"]
        v_up = b.cnt["pte"]
        b.W("se", "pte", v_up)
        for half in range(2):
            b.se("activation", gsil[:, half * 512:(half + 1) * 512],
                 pb[2 + half][:, 0:512], AF.Silu,
                 scale=P["sug"][:, 64 + a:64 + a + 1])
        ev_upbank = b.cnt["pse"]
        v_sil = b.cnt["pse"]
        b.W("ve", "pse", v_sil)
        for half in range(2):
            b.ve("scalar_tensor_tensor", gtile[:, half * 512:(half + 1) * 512],
                 pb[half][:, 0:512], P["sug"][:, a:a + 1],
                 gsil[:, half * 512:(half + 1) * 512],
                 op0=ALU.mult, op1=ALU.mult)
        ev_gdone = b.cnt["pve"]
        b.W("te", "pve", ev_gdone)
        b.W("te", "pve", ev_ybank)
        for half in range(2):
            b.mm(pb[4 + half][:, 0:512], hsud(s),
                 gtile[:, half * 512:(half + 1) * 512], inc=(half == 1))
        ev_hsu[s] = b.cnt["pte"]
        v_y = b.cnt["pte"]
        b.W("ve", "pte", v_y)
        ys = a % 2
        b.W("ve", f"s_y{ys}", ev_ysb.get(ys, 0))
        b.ve("tensor_copy", ytile(ys)[:, 0:512], pb[4][:, 0:512])
        vy2 = b.ve("tensor_copy", ytile(ys)[:, 512:1024], pb[5][:, 0:512])
        ev_ybank = vy2
        b.W("ge", "pve", vy2)
        b.dma("ge", y_d[a], ytile(ys), sem=f"s_y{ys}")
        ev_ysb[ys] = b.cnt[f"s_y{ys}"]
    if DEBUG:
        b.barrier()
        b.dma("ge", dbg["d_y"], y_d)

    b.phase("P2")
    b.barrier()
    p2a = lambda a: Z0(a * 256, 256)
    p2b = lambda a: Z1(a * 256, 256)
    v_wr = 0
    for tc in range(4):
        if tc > 0:
            b.W("sy", "dge", v_wr)
            b.W("ge", "dge", v_wr)
        for a in range(64):
            b.dma("sy" if a % 2 == 0 else "ge", p2a(a),
                  y_d[a, :, tc * 256:(tc + 1) * 256])
        v_ld_sy = b.cnt["dsy"]
        v_ld_ge = b.cnt["dge"]
        b.W("ve", "dsy", v_ld_sy)
        b.W("ve", "dge", v_ld_ge)
        for stage, hbit in enumerate([1, 2, 4, 8, 16, 32]):
            src = p2a if stage % 2 == 0 else p2b
            dst = p2b if stage % 2 == 0 else p2a
            for i in range(64):
                jlo, jhi = i & ~hbit, i | hbit
                op = ALU.add if (i & hbit) == 0 else ALU.subtract
                b.ve("tensor_tensor", dst(i), src(jlo), src(jhi), op=op)
        v_bf = b.cnt["pve"]
        b.W("ge", "pve", v_bf)
        for a in range(64):
            b.dma("ge", x2_d[a, :, tc * 256:(tc + 1) * 256], p2a(a))
        v_wr = b.cnt["dge"]
    if DEBUG:
        b.barrier()
        b.dma("ge", dbg["d_x2"], x2_d)

    b.phase("P3")
    b.barrier()
    acc = lambda m: Z0(m * 1024, 1024)
    wdb = lambda s: Z1(s * 512, 512)
    x2c = lambda s: Z1(2048 + s * 1024, 1024)
    ev_slot3 = {}
    ev_p3bank = 0
    for sw in range(4):
        for k in range(64):
            s = k % 4
            b.W("ge", "pte", ev_slot3.get(s, 0))
            b.dma("ge", wdb(s), wdown_d[sw, k], sem=f"s_d{s}")
            vw = b.cnt[f"s_d{s}"]
            b.W("sy", "pte", ev_slot3.get(s, 0))
            b.dma("sy", x2c(s), x2_d[k], sem=f"s_x{s}")
            vx = b.cnt[f"s_x{s}"]
            b.W("te", f"s_d{s}", vw)
            b.W("te", f"s_x{s}", vx)
            if k == 0:
                b.W("te", "pve", ev_p3bank)
            for j in range(4):
                for half in range(2):
                    b.mm(pb[j * 2 + half][:, 0:512],
                         wdb(s)[:, j * 128:(j + 1) * 128],
                         x2c(s)[:, half * 512:(half + 1) * 512],
                         start=(k == 0), stop=(k == 63),
                         inc=(j == 3 and half == 1))
            ev_slot3[s] = b.cnt["pte"]
        v_p3 = b.cnt["pte"]
        b.W("ve", "pte", v_p3)
        for j in range(4):
            for half in range(2):
                b.ve("tensor_scalar_mul",
                     acc(sw * 4 + j)[:, half * 512:(half + 1) * 512],
                     pb[j * 2 + half][:, 0:512],
                     P["sdp"][:, sw * 4 + j:sw * 4 + j + 1])
        ev_p3bank = b.cnt["pve"]

    b.phase("PD")
    b.barrier()
    wob = lambda s: Z1(6144 + s * 1024, 1024)
    ev_wo = {}
    ev_wobank = 0
    for m in range(16):
        s = m % 2
        b.W("ge", "pte", ev_wo.get(s, 0))
        b.dma("ge", wob(s), wo_d[m], sem=f"s_o{s}")
        vw = b.cnt[f"s_o{s}"]
        b.W("te", f"s_o{s}", vw)
        b.W("te", "pve", ev_wobank)
        for c in range(8):
            for half in range(2):
                b.mm(pb[half][:, 0:512], wob(s)[:, c * 128:(c + 1) * 128],
                     P["attnv"][:, c * TO + half * 512: c * TO + (half + 1) * 512],
                     start=(c == 0), stop=(c == 7),
                     inc=(c == 7 and half == 1))
        ev_wo[s] = b.cnt["pte"]
        v_wo = b.cnt["pte"]
        b.W("ve", "pte", v_wo)
        for half in range(2):
            b.ve("tensor_tensor", acc(m)[:, half * 512:(half + 1) * 512],
                 acc(m)[:, half * 512:(half + 1) * 512], pb[half][:, 0:512],
                 op=ALU.add)
        ev_wobank = b.cnt["pve"]
    if DEBUG:
        b.barrier()
        b.dma("ge", dbg["d_acc"], Z0(0, 16384))

    b.phase("PF")
    b.barrier()
    xob = lambda s: Z1(8192 + s * 2048, 2048)
    outb = lambda s: Z1(12288 + s * 2048, 2048)
    ev_xo = {}
    ev_outb = {}
    ev_fbank = {}
    for tt in range(NQT):
        s = tt % 2
        b.W("sy", "pve", ev_xo.get(s, 0))
        b.dma("sy", xob(s), xo[tt * 128:(tt + 1) * 128, :], sem=f"s_f{s}")
        v_x = b.cnt[f"s_f{s}"]
        for cg in range(4):
            bank = cg % 2
            b.W("te", "pve", ev_fbank.get(bank, 0))
            for cc in range(4):
                c = cg * 4 + cc
                b.tr(pb[bank][:, cc * 128:(cc + 1) * 128],
                     acc(c)[:, tt * 128:(tt + 1) * 128], P["idn"])
            v_t = b.cnt["pte"]
            b.W("ve", "pte", v_t)
            b.W("ve", f"s_f{s}", v_x)
            if cg == 0:
                b.W("ve", f"s_b{s}", ev_outb.get(s, 0))
            vv = b.ve("tensor_tensor", outb(s)[:, cg * 512:(cg + 1) * 512],
                      xob(s)[:, cg * 512:(cg + 1) * 512], pb[bank][:, 0:512],
                      op=ALU.add)
            ev_fbank[bank] = vv
        ev_xo[s] = b.cnt["pve"]
        b.W("ge", "pve", ev_xo[s])
        b.dma("ge", out_d[tt * 128:(tt + 1) * 128, :], outb(s), sem=f"s_b{s}")
        ev_outb[s] = b.cnt[f"s_b{s}"]
    if DEBUG2:
        b.barrier()
        b.dma("ge", dbg2["e_attnv"], P["attnv"])
        b.dma("ge", dbg2["e_y"], y_d)
        b.dma("ge", dbg2["e_x2"], x2_d)
        b.dma("ge", dbg2["e_xn"], xn_d)
    for s_, v_ in list(b.cnt.items()):
        if s_.startswith(("d", "s_")) and v_ > 0:
            b.W("ge", s_, v_)

    b.finish()
    return nc


def _host_prep(inputs):
    f32 = np.float32
    x = np.ascontiguousarray(np.asarray(inputs["L0_attn_x"], f32).reshape(T, D))
    WqS = np.asarray(inputs["L0_attn_Wq_sem"], f32)
    WkS = np.asarray(inputs["L0_attn_Wk_sem"], f32)
    WqG = np.asarray(inputs["L0_attn_Wq_geo"], f32)
    WkG = np.asarray(inputs["L0_attn_Wk_geo"], f32)
    Wv = np.asarray(inputs["L0_attn_Wv"], f32)
    Wo = np.asarray(inputs["L0_attn_Wo"], f32)
    ls = np.asarray(inputs["L0_attn_logit_scale"], f32)
    SUu = np.asarray(inputs["L0_ffn_SU_ug"], f32)
    SUd = np.asarray(inputs["L0_ffn_SU_dp"], f32)
    WSu = np.asarray(inputs["L0_ffn_Wscale_ug"], f32)
    WSd = np.asarray(inputs["L0_ffn_Wscale_dp"], f32)
    Wup = np.asarray(inputs["L0_ffn_W_up"], f32)
    Wg = np.asarray(inputs["L0_ffn_W_gate"], f32)
    Wd = np.asarray(inputs["L0_ffn_W_down"], f32)

    Wq_hat = np.empty((D, 1024), f32)
    for h in range(H):
        Wq_hat[:, h * 64:h * 64 + 32] = WqS[:, h * 32:(h + 1) * 32]
        Wq_hat[:, h * 64 + 32:h * 64 + 64] = WqG[:, h * 32:(h + 1) * 32]
    Wk_hat = np.empty((D, 256), f32)
    for g in range(HKV):
        Wk_hat[:, g * 64:g * 64 + 32] = WkS[:, g * 32:(g + 1) * 32]
        Wk_hat[:, g * 64 + 32:g * 64 + 64] = WkG[:, g * 32:(g + 1) * 32]
    Wkv_hat = np.concatenate([Wk_hat, Wv], axis=1)

    wq = np.ascontiguousarray(Wq_hat.reshape(16, 128, 1024))
    wkv = np.ascontiguousarray(
        Wkv_hat.reshape(16, 128, 4, 128).transpose(1, 0, 2, 3).reshape(128, 8192))
    wup = np.ascontiguousarray(
        Wup.T.reshape(16, 128, 64, 128).transpose(2, 1, 0, 3).reshape(64, 128, 2048))
    wgate = np.ascontiguousarray(
        Wg.T.reshape(16, 128, 64, 128).transpose(2, 1, 0, 3).reshape(64, 128, 2048))
    wdown = np.ascontiguousarray(
        Wd.T.reshape(64, 128, 4, 512).transpose(2, 0, 1, 3))
    wo = np.ascontiguousarray(
        Wo.reshape(8, 128, 16, 128).transpose(2, 1, 0, 3).reshape(16, 128, 1024))

    inv = (10000.0 ** -(np.arange(16) / 16.0)).astype(np.float64)
    ang = np.arange(T, dtype=np.float64)[:, None] * inv[None, :]
    cosT, sinT = np.cos(ang).astype(f32), np.sin(ang).astype(f32)
    A = np.zeros((128, T), f32)
    Btab = np.zeros((128, T), f32)
    for blk in range(2):
        r0 = blk * 64
        A[r0:r0 + 32, :] = 1.0
        for i in range(16):
            A[r0 + 32 + i, :] = cosT[:, i]
            A[r0 + 48 + i, :] = cosT[:, i]
            Btab[r0 + 32 + i, :] = -sinT[:, i]
            Btab[r0 + 48 + i, :] = sinT[:, i]
    ropek = np.ascontiguousarray(
        np.stack([A, Btab], 0).reshape(2, 128, NHG, HG).transpose(2, 0, 1, 3))

    k_pos = np.concatenate([np.arange(NBLK) * MBLK + (MBLK - 1),
                            np.arange(REMOTE, T),
                            np.full(NKEYP - NKEY, 2 * T)]).astype(np.int64)
    ispad = np.arange(NKEYP) >= NKEY

    in_maps = []
    for c in range(NCORE):
        q = np.arange(c * TO, (c + 1) * TO)
        vis = k_pos[None, :] <= q[:, None]
        cap = np.where(vis, POS,
                       np.where(ispad[None, :], NEGPAD, NEG)).astype(f32)
        im = {
            "xf": x, "xo": np.ascontiguousarray(x[c * TO:(c + 1) * TO]),
            "wq": wq, "wkv": wkv, "wup": wup, "wgate": wgate,
            "wdown": wdown, "wo": wo,
            "su_ug": np.ascontiguousarray(SUu.reshape(16, 128)),
            "su_dp": np.ascontiguousarray(SUd.reshape(64, 128)),
            "wsc_ug": np.ascontiguousarray(WSu.reshape(128, 128)),
            "wsc_dp": np.ascontiguousarray(WSd.reshape(16, 128)),
            "ls": np.ascontiguousarray(ls.reshape(1, 16)),
            "mask": np.ascontiguousarray(cap.reshape(NQT, 128, NKEYP)),
            "ropeq": np.ascontiguousarray(
                np.stack([A, Btab], 0)[:, :, c * TO:(c + 1) * TO]),
            "ropek": ropek,
        }
        in_maps.append(im)
    return in_maps


def kernel(**inputs):
    in_maps = _host_prep(inputs)
    nc = build_program()
    res = run_bass_kernel_spmd(nc, in_maps, core_ids=list(range(NCORE)))
    out = np.concatenate([r["out"] for r in res.results], axis=0)
    return out.reshape(1, T, D).astype(np.float32)


# version 03
# speedup vs baseline: 1.0643x; 1.0643x over previous
import numpy as np

import concourse.bass as bass
import concourse.mybir as mybir
from concourse.bass_utils import run_bass_kernel_spmd

F32 = mybir.dt.float32
BF16 = mybir.dt.bfloat16
AF = mybir.ActivationFunctionType
ALU = mybir.AluOpType
AX = mybir.AxisListType

NCORE = 8
T, D, INTER = 8192, 2048, 8192
TO = T // NCORE
H, HKV, DV = 16, 4, 64
REMOTE, MBLK, LW = 7168, 64, 1024
NBLK = REMOTE // MBLK
NKEY = NBLK + LW
NKEYP = 1152
NEG, NEGPAD, POS = -1.0e9, -2.0e9, 1.0e30
HG = 256
NHG = T // HG
NOG = TO // HG
NQT = TO // 128
EPS = 1e-5

DEBUG = False
DEBUG2 = False
PE1_BARRIER = False
PHASES = None
ENGINES = ("te", "ve", "se", "ge", "sy")
SHUF16 = [(i + 16) % 32 for i in range(32)]


def _had(n):
    h = np.array([[1.0]])
    while h.shape[0] < n:
        h = np.block([[h, h], [h, -h]])
    return h


class B:
    def __init__(self, nc):
        self.nc = nc
        self.ops = {e: [] for e in ENGINES}
        self.cnt = {}
        self.sems = {}
        self.enabled = True


    def phase(self, name):
        self.enabled = (PHASES is None) or (name in PHASES)

    def sem(self, name):
        if name not in self.sems:
            self.sems[name] = self.nc.alloc_semaphore(name)
            self.cnt[name] = 0
        return self.sems[name]

    def W(self, eng, sem, val):
        if not self.enabled:
            return
        if not val or val <= 0:
            return
        self.sem(sem)
        self.ops[eng].append(("w", sem, val))

    def _emit(self, eng, meth, args, kw, inc):
        if not self.enabled:
            return self.cnt.get(inc[0], 0) if inc is not None else None
        if inc is not None:
            self.sem(inc[0])
            self.cnt[inc[0]] += inc[1]
        self.ops[eng].append(("e", meth, args, kw, inc))
        return self.cnt[inc[0]] if inc is not None else None

    def ve(self, meth, *a, **kw):
        self.W("ve", "pve", self.cnt.get("pve", 0))
        return self._emit("ve", meth, a, kw, ("pve", 1))

    def se(self, meth, *a, **kw):
        self.W("se", "pse", self.cnt.get("pse", 0))
        return self._emit("se", meth, a, kw, ("pse", 1))

    def ge(self, meth, *a, **kw):
        self.W("ge", "pge", self.cnt.get("pge", 0))
        return self._emit("ge", meth, a, kw, ("pge", 1))

    def mm(self, out, lhsT, rhs, start=True, stop=True, inc=True, **kw):
        return self._emit("te", "matmul", (out, lhsT, rhs),
                          dict(start=start, stop=stop, **kw),
                          ("pte", 1) if inc else None)

    def tr(self, out, in_, ident):
        return self._emit("te", "matmul", (out, in_, ident),
                          dict(is_transpose=True), ("pte", 1))

    def dma(self, q, out, in_, sem=None):
        return self._emit(q, "dma_start", (), dict(out=out, in_=in_),
                          (sem or ("d" + q), 16))

    def barrier(self):
        snap = [(s, v) for s, v in self.cnt.items() if v > 0]
        for e in ENGINES:
            for s, v in snap:
                self.W(e, s, v)

    def finish(self):
        with self.nc.Block() as block:
            regs = {"sy": block.sync, "te": block.tensor, "ve": block.vector,
                    "se": block.scalar, "ge": block.gpsimd}
            for name in ENGINES:
                lst = self.ops[name]

                def body(E, lst=lst):
                    for op in lst:
                        if op[0] == "w":
                            E.wait_ge(self.sems[op[1]], op[2])
                        else:
                            _, meth, args, kw, inc = op
                            i = getattr(E, meth)(*args, **kw)
                            if inc is not None:
                                i.then_inc(self.sems[inc[0]], inc[1])
                regs[name](body)


def build_program():
    nc = bass.Bass("TRN2", target_bir_lowering=False, debug=False,
                   detect_race_conditions=False)
    b = B(nc)
    for s_ in ("pte", "pve", "pse", "pge", "dsy", "dge", "dte", "dve", "dse"):
        b.sem(s_)

    din = lambda n, s: nc.dram_tensor(n, s, F32, kind="ExternalInput").ap()
    xf = din("xf", [T, D])
    xo = din("xo", [TO, D])
    wq_d = din("wq", [16, 128, 1024])
    wkv_d = din("wkv", [128, 8192])
    wup_d = din("wup", [64, 128, 2048])
    wgate_d = din("wgate", [64, 128, 2048])
    wdown_d = din("wdown", [4, 64, 128, 512])
    wo_d = din("wo", [16, 128, 1024])
    suu_d = din("su_ug", [16, 128])
    sud_d = din("su_dp", [64, 128])
    sug_d = din("wsc_ug", [128, 128])
    sdp_d = din("wsc_dp", [16, 128])
    ls_d = din("ls", [1, 16])
    mask_d = din("mask", [NQT, 128, NKEYP])
    ropeq_d = din("ropeq", [2, 128, TO])
    ropek_d = din("ropek", [NHG, 2, 128, HG])
    out_d = nc.dram_tensor("out", [TO, D], F32, kind="ExternalOutput").ap()

    hc_d = nc.inline_tensor((_had(128) / 64.0).astype(np.float32),
                            name="hado64").ap()
    id_d = nc.inline_tensor(np.eye(128, dtype=np.float32), name="identc").ap()
    ones_d = nc.inline_tensor(np.ones((1, 128), dtype=np.float32),
                              name="onesr").ap()
    cv_d = nc.inline_tensor(np.array([[32.0, 2048.0, 8192.0, 1.0]],
                                     dtype=np.float32), name="cvals").ap()
    eps_d = nc.inline_tensor(np.full((1, 1), EPS, dtype=np.float32),
                             name="epsc").ap()
    import ml_dtypes
    hcb_d = nc.inline_tensor((_had(128) / 64.0).astype(ml_dtypes.bfloat16),
                             name="hadb").ap()

    xn_dh = nc.dram_tensor("xn_sh", [16, 128, TO], BF16, kind="Internal").ap()
    xn_dl = nc.dram_tensor("xn_sl", [16, 128, TO], BF16, kind="Internal").ap()
    y_d = nc.dram_tensor("y_s", [64, 128, TO], F32, kind="Internal").ap()
    x2_dh = nc.dram_tensor("x2_sh", [64, 128, TO], BF16, kind="Internal").ap()
    x2_dl = nc.dram_tensor("x2_sl", [64, 128, TO], BF16, kind="Internal").ap()
    wq_bh = nc.dram_tensor("wq_bh", [16, 128, 1024], BF16, kind="Internal").ap()
    wq_bl = nc.dram_tensor("wq_bl", [16, 128, 1024], BF16, kind="Internal").ap()
    wo_bh = nc.dram_tensor("wo_bh", [16, 128, 1024], BF16, kind="Internal").ap()
    wo_bl = nc.dram_tensor("wo_bl", [16, 128, 1024], BF16, kind="Internal").ap()
    wup_bh = nc.dram_tensor("wup_bh", [64, 128, 2048], BF16, kind="Internal").ap()
    wup_bl = nc.dram_tensor("wup_bl", [64, 128, 2048], BF16, kind="Internal").ap()
    wg_bh = nc.dram_tensor("wg_bh", [64, 128, 2048], BF16, kind="Internal").ap()
    wg_bl = nc.dram_tensor("wg_bl", [64, 128, 2048], BF16, kind="Internal").ap()
    wd_bh = nc.dram_tensor("wd_bh", [4, 64, 128, 512], BF16, kind="Internal").ap()
    wd_bl = nc.dram_tensor("wd_bl", [4, 64, 128, 512], BF16, kind="Internal").ap()

    perm = {}
    for name, shape in [
            ("hc", [128, 128]), ("idn", [128, 128]), ("ones", [1, 128]),
            ("cv", [1, 4]), ("rcv", [1, 4]), ("rcvb", [128, 4]),
            ("suu", [128, 16]), ("sud", [128, 64]),
            ("sug", [128, 128]), ("sdp", [128, 16]),
            ("lse", [1, 16]), ("qsc", [128, 16]), ("smv", [128, 8]),
            ("epsb", [128, 1]), ("eps1", [1, 1]),

            ("kp", [128, 2 * NKEYP]), ("kps", [128, 2 * NKEYP]),
            ("vp", [128, 2 * NKEYP]),
            ("vpt", [128, 9 * 256]),
            ("attnv", [128, 8 * TO]),
    ]:
        perm[name] = nc.alloc_sbuf_tensor(name, shape, F32)
    perm["hcb"] = nc.alloc_sbuf_tensor("hcb", [128, 128], BF16)
    perm["hsu_ug"] = nc.alloc_sbuf_tensor("hsu_ug", [128, 2048], BF16)
    z0 = nc.alloc_sbuf_tensor("z0", [128, 16384], F32)
    z1 = nc.alloc_sbuf_tensor("z1", [128, 16384], F32)
    Z0 = lambda off, n, p=128: z0[0:p, off:off + n]
    Z1 = lambda off, n, p=128: z1[0:p, off:off + n]
    P = {k: v[:] for k, v in perm.items()}

    pb = [nc.alloc_psum_tensor(f"pb{i}", [128, 512], F32) for i in range(8)]

    dbg2 = {}
    if DEBUG2:
        for name, shape in [("e_attnv", [128, 8 * TO]), ("e_y", [64, 128, TO]),
                            ("e_x2", [64, 128, TO]), ("e_xn", [16, 128, TO]),
                            ("e_x1", [128, 16384])]:
            dbg2[name] = nc.dram_tensor(name, shape, F32,
                                        kind="ExternalOutput").ap()
    dbg = {}
    if DEBUG:
        for name, shape in [("d_kp", [128, 2 * NKEYP]), ("d_vp", [128, 2 * NKEYP]),
                            ("d_qhat", [128, 8192]), ("d_attnv", [128, 8 * TO]),
                            ("d_x1", [128, 16384]), ("d_acc", [128, 16384]),
                            ("d_xn", [16, 128, TO]), ("d_vpt", [128, 9 * 256]),
                            ("d_y", [64, 128, TO]), ("d_x2", [64, 128, TO])]:
            dbg[name] = nc.dram_tensor(name, shape, F32,
                                       kind="ExternalOutput").ap()

    b.phase("P0")
    b.dma("sy", P["hc"], hc_d)
    b.dma("sy", P["idn"], id_d)
    b.dma("sy", P["ones"], ones_d)
    b.dma("sy", P["cv"], cv_d)
    b.dma("sy", P["eps1"], eps_d)
    b.dma("sy", P["lse"], ls_d)
    st_suu = Z1(4608, 128, 16)
    st_sud = Z1(4736, 128, 64)
    st_sug = Z1(4864, 128, 128)
    st_sdp = Z1(4992, 128, 16)
    b.dma("sy", st_suu, suu_d)
    b.dma("sy", st_sud, sud_d)
    b.dma("sy", st_sug, sug_d)
    b.dma("sy", st_sdp, sdp_d)
    b.dma("sy", P["hcb"], hcb_d)
    b.dma("sy", Z0(0, 8192), wkv_d)
    v_c = b.cnt["dsy"]

    b.W("se", "dsy", v_c)
    b.se("activation", P["rcv"], P["cv"], AF.Sqrt)
    vse = b.se("activation", P["lse"], P["lse"], AF.Exp)
    b.W("ve", "pse", vse)
    b.ve("reciprocal", P["rcv"], P["rcv"])
    vve = b.ve("tensor_scalar_mul", P["lse"], P["lse"], P["rcv"][0:1, 0:1])
    b.W("te", "pve", vve)
    b.W("te", "dsy", v_c)
    b.mm(pb[0][:, 0:4], P["ones"], P["rcv"], inc=False)
    b.mm(pb[0][:, 8:9], P["ones"], P["eps1"], inc=False)
    b.mm(pb[0][:, 16:32], P["ones"], P["lse"], inc=False)
    b.tr(pb[1][:, 0:16], st_suu, P["idn"][0:16, 0:16])
    b.tr(pb[1][:, 16:80], st_sud, P["idn"][0:64, 0:64])
    b.tr(pb[2][:, 0:128], st_sug, P["idn"])
    vte = b.tr(pb[1][:, 80:96], st_sdp, P["idn"][0:16, 0:16])
    b.W("ve", "pte", vte)
    b.ve("tensor_copy", P["rcvb"], pb[0][:, 0:4])
    b.ve("tensor_copy", P["epsb"], pb[0][:, 8:9])
    b.ve("tensor_copy", P["qsc"], pb[0][:, 16:32])
    b.ve("tensor_copy", P["suu"], pb[1][:, 0:16])
    b.ve("tensor_copy", P["sud"], pb[1][:, 16:80])
    b.ve("tensor_copy", P["sdp"], pb[1][:, 80:96])
    b.ve("tensor_copy", P["sug"], pb[2][:, 0:128])
    for m in range(8):
        b.ve("tensor_copy", P["smv"][0:64, m:m + 1],
             P["qsc"][0:64, 2 * m:2 * m + 1])
        b.ve("tensor_copy", P["smv"][64:128, m:m + 1],
             P["qsc"][64:128, 2 * m + 1:2 * m + 2])
    b.ve("tensor_scalar_mul", P["sug"], P["sug"], 64.0)
    b.ve("tensor_scalar_mul", P["sug"], P["sug"], P["rcvb"][:, 1:2])
    b.ve("tensor_scalar_mul", P["sdp"], P["sdp"], 64.0)
    b.ve("tensor_scalar_mul", P["sdp"], P["sdp"], P["rcvb"][:, 2:3])
    for a in range(16):
        b.ve("tensor_scalar_mul", P["hsu_ug"][:, a * 128:(a + 1) * 128],
             P["hcb"], P["suu"][:, a:a + 1])
    b.ve("memset", P["kp"], 0.0)
    b.ve("memset", P["vp"], 0.0)
    wkvh = Z0(8192, 4096).bitcast(BF16)
    wkvl = Z0(12288, 4096).bitcast(BF16)
    vh_ = b.se("copy", wkvh, Z0(0, 8192))
    b.W("ve", "pse", vh_)
    b.ve("tensor_tensor", wkvl, Z0(0, 8192), wkvh, op=ALU.subtract)
    b.barrier()

    b.phase("PA")
    xtm = lambda tt: Z0(tt * 2048, 2048)
    xngh = lambda c: Z0(4096 + c * 128, 128).bitcast(BF16)
    xngl = lambda c: Z0(6144 + c * 128, 128).bitcast(BF16)
    sqscr = Z1(0, 2048)
    stc = lambda tt, j: Z1(2048 + tt * 8 + j, 1)
    khg = lambda kt: Z1(2112 + kt * 256, 256)
    khg2 = lambda kt: Z1(2624 + kt * 256, 256)
    vhg = lambda kt: Z1(3136 + kt * 256, 256)
    rkA = Z1(3648, 256)
    rkB = Z1(3904, 256)
    pool_a = Z1(4416, 128)
    pool_b = Z1(4544, 64)
    ropet = Z1(5120, 256)

    def stats_norm(xt_ap, tt):
        s1, s2 = stc(tt, 0), stc(tt, 1)
        m, e2, msq, var, r = (stc(tt, 2), stc(tt, 3), stc(tt, 4), stc(tt, 5),
                              stc(tt, 6))
        vs = b.se("activation", sqscr, xt_ap, AF.Square, accum_out=s2)
        b.ve("reduce_sum", s1, xt_ap, axis=AX.X)
        b.ve("tensor_scalar_mul", m, s1, 1.0 / 2048.0)
        b.W("ve", "pse", vs)
        b.ve("tensor_scalar_mul", e2, s2, 1.0 / 2048.0)
        b.ve("tensor_scalar_mul", msq, m, m)
        vv = b.ve("tensor_tensor", var, e2, msq, op=ALU.subtract)
        b.W("se", "pve", vv)
        b.se("activation", r, var, AF.Ln, bias=P["epsb"])
        vs2 = b.se("activation", r, r, AF.Exp, scale=-0.5)
        b.W("ve", "pse", vs2)
        return b.ve("tensor_scalar", xt_ap, xt_ap, m, r,
                    op0=ALU.subtract, op1=ALU.mult)

    def rope(dst, src, A_ap, B_ap, tmp, width):
        b.ve("stream_shuffle", tmp[:, 0:width], src, SHUF16)
        b.ve("tensor_tensor", dst, src, A_ap[:, 0:width], op=ALU.mult)
        b.ve("tensor_tensor", tmp[:, 0:width], tmp[:, 0:width], B_ap[:, 0:width],
             op=ALU.mult)
        return b.ve("tensor_tensor", dst, dst, tmp[:, 0:width], op=ALU.add)

    def pool64(src, dst_ap):
        seq = [(src, 256, pool_a), (pool_a, 128, pool_b), (pool_b, 64, pool_a),
               (pool_a, 32, pool_b), (pool_b, 16, pool_a), (pool_a, 8, pool_b)]
        for s_ap, w, d_ap in seq:
            v = s_ap[:, 0:w].rearrange("p (b w) -> p b w", b=4)
            hw = w // 8
            b.ve("tensor_tensor",
                 d_ap[:, 0:w // 2].rearrange("p (b w) -> p b w", b=4),
                 v[:, :, 0:hw], v[:, :, hw:2 * hw], op=ALU.add)
        return b.ve("tensor_scalar_mul", dst_ap, pool_b[:, 0:4], 1.0 / 64.0)

    conv_units = []
    for i in range(16):
        conv_units.append((wq_d[i], wq_bh[i], wq_bl[i], 1024))
    for i in range(16):
        conv_units.append((wo_d[i], wo_bh[i], wo_bl[i], 1024))
    for i in range(64):
        conv_units.append((wup_d[i], wup_bh[i], wup_bl[i], 2048))
        conv_units.append((wgate_d[i], wg_bh[i], wg_bl[i], 2048))
    for sw_ in range(4):
        for k_ in range(64):
            conv_units.append((wdown_d[sw_, k_], wd_bh[sw_, k_],
                               wd_bl[sw_, k_], 512))
    conv_idx = [0]
    ev_cstage = {}

    def conv_step(n_units):
        for _ in range(n_units):
            i = conv_idx[0]
            if i >= len(conv_units):
                return
            conv_idx[0] += 1
            src, dsth, dstl, w = conv_units[i]
            cs = i % 2
            stage = Z1(5376 + cs * 2048, w)
            outh = Z1(9472 + cs * 1024, w // 2).bitcast(BF16)
            outl = Z1(11520 + cs * 1024, w // 2).bitcast(BF16)
            b.W("sy", "pge", ev_cstage.get(cs, 0))
            b.dma("sy", stage, src, sem=f"s_pc{cs}")
            vst = b.cnt[f"s_pc{cs}"]
            b.W("se", f"s_pc{cs}", vst)
            b.W("se", f"s_ph{cs}", b.cnt.get(f"s_ph{cs}", 0))
            vhi = b.se("copy", outh, stage)
            b.W("ge", "pse", vhi)
            b.W("ge", f"s_pl{cs}", b.cnt.get(f"s_pl{cs}", 0))
            vlo = b.ge("tensor_tensor", outl, stage, outh, op=ALU.subtract)
            ev_cstage[cs] = vlo
            b.W("ge", "pge", vlo)
            b.dma("ge", dsth, outh, sem=f"s_ph{cs}")
            b.dma("ge", dstl, outl, sem=f"s_pl{cs}")

    ev_xtm_free = 0
    ev_rk_free = 0
    ev_xng = {}
    ev_tbank = {}
    ev_kvbank = 0
    for hg in range(NHG):
        conv_step(13)
        b.W("sy", "pve", ev_xtm_free)
        for tt in range(2):
            b.dma("sy", xtm(tt),
                  xf[hg * HG + tt * 128: hg * HG + (tt + 1) * 128, :],
                  sem="s_xa")
        v_x = b.cnt["s_xa"]
        b.W("sy", "pve", ev_rk_free)
        b.dma("sy", rkA, ropek_d[hg, 0], sem="s_rk")
        b.dma("sy", rkB, ropek_d[hg, 1], sem="s_rk")
        v_rk = b.cnt["s_rk"]

        b.W("se", "s_xa", v_x)
        b.W("ve", "s_xa", v_x)
        norm_ev = [stats_norm(xtm(tt), tt) for tt in range(2)]
        b.W("te", "pve", norm_ev[1])
        for c in range(16):
            bank = c % 2
            b.W("te", "pve", ev_tbank.get(bank, 0))
            for tt in range(2):
                b.tr(pb[bank][:, tt * 128:(tt + 1) * 128],
                     xtm(tt)[:, c * 128:(c + 1) * 128], P["idn"])
            vt = b.cnt["pte"]
            b.W("se", "pte", vt)
            vhi = b.se("copy", xngh(c), pb[bank][:, 0:256])
            b.W("ve", "pse", vhi)
            ev_xng[c] = b.ve("tensor_tensor", xngl(c), pb[bank][:, 0:256],
                             xngh(c), op=ALU.subtract)
            ev_tbank[bank] = ev_xng[c]
        ev_xtm_free = ev_xng[15]

        b.W("te", "pve", ev_xng[15])
        b.W("te", "pve", ev_kvbank)
        for c in range(16):
            for w in range(4):
                wslice = lambda wt, cc=c, ww=w: wt[:, (cc * 4 + ww) * 128:
                                                  (cc * 4 + ww + 1) * 128]
                b.mm(pb[4 + w][:, 0:256], wslice(wkvh), xngh(c),
                     start=(c == 0), stop=False, inc=False)
                b.mm(pb[4 + w][:, 0:256], wslice(wkvh), xngl(c),
                     start=False, stop=False, inc=False)
                b.mm(pb[4 + w][:, 0:256], wslice(wkvl), xngh(c),
                     start=False, stop=(c == 15),
                     inc=(c == 15 and w == 3))
        v_kv = b.cnt["pte"]
        b.W("ve", "pte", v_kv)
        b.W("ve", "s_rk", v_rk)
        for kt in range(2):
            b.ve("tensor_copy", khg(kt), pb[4 + kt][:, 0:256])
            b.ve("tensor_copy", vhg(kt), pb[6 + kt][:, 0:256])
        for kt in range(2):
            rope(khg2(kt), khg(kt), rkA, rkB, ropet, 256)
        ev_rk_free = b.cnt["pve"]
        if hg < REMOTE // HG:
            for kt in range(2):
                pool64(khg2(kt),
                       P["kp"][:, kt * NKEYP + hg * 4: kt * NKEYP + hg * 4 + 4])
                pool64(vhg(kt),
                       P["vp"][:, kt * NKEYP + hg * 4: kt * NKEYP + hg * 4 + 4])
        else:
            off = NBLK + (hg - REMOTE // HG) * HG
            for kt in range(2):
                b.ve("tensor_copy",
                     P["kp"][:, kt * NKEYP + off: kt * NKEYP + off + HG],
                     khg2(kt))
                b.ve("tensor_copy",
                     P["vp"][:, kt * NKEYP + off: kt * NKEYP + off + HG],
                     vhg(kt))
        ev_kvbank = b.cnt["pve"]
    conv_step(1000)

    if DEBUG:
        b.barrier()
        b.dma("ge", dbg["d_kp"], P["kp"])
        b.dma("ge", dbg["d_vp"], P["vp"])

    b.phase("PA2")
    b.barrier()
    ev_xtm_free = 0
    ev_tbank = {}
    ev_xnd = {}
    for og in range(NOG):
        b.W("sy", "pve", ev_xtm_free)
        for tt in range(2):
            b.dma("sy", xtm(tt),
                  xo[og * HG + tt * 128: og * HG + (tt + 1) * 128, :],
                  sem="s_xa")
        v_x = b.cnt["s_xa"]
        b.W("se", "s_xa", v_x)
        b.W("ve", "s_xa", v_x)
        norm_ev = [stats_norm(xtm(tt), tt) for tt in range(2)]
        b.W("te", "pve", norm_ev[1])
        b.W("te", "s_xw", 32 * 16 * og)
        for c in range(16):
            bank = c % 2
            b.W("te", "pve", ev_tbank.get(bank, 0))
            for tt in range(2):
                b.tr(pb[bank][:, tt * 128:(tt + 1) * 128],
                     xtm(tt)[:, c * 128:(c + 1) * 128], P["idn"])
            vt = b.cnt["pte"]
            b.W("se", "pte", vt)
            vhi = b.se("copy", xngh(c), pb[bank][:, 0:256])
            b.W("ve", "pse", vhi)
            vv = b.ve("tensor_tensor", xngl(c), pb[bank][:, 0:256],
                      xngh(c), op=ALU.subtract)
            ev_tbank[bank] = vv
            b.W("sy", "pve", vv)
            b.dma("sy", xn_dh[c, :, og * HG:(og + 1) * HG], xngh(c), sem="s_xw")
            b.dma("sy", xn_dl[c, :, og * HG:(og + 1) * HG], xngl(c), sem="s_xw")
        ev_xtm_free = ev_tbank[1]
    if DEBUG:
        b.barrier()
        b.dma("ge", dbg["d_xn"], xn_dh)

    b.phase("VPT")
    b.barrier()
    b.ve("tensor_copy", P["kps"][0:64, :], P["kp"][64:128, :])
    b.ve("tensor_copy", P["kps"][64:128, :], P["kp"][0:64, :])
    ev_vb = {}
    for kc in range(9):
        for kt in range(2):
            bank = (kc * 2 + kt) % 2
            b.W("te", "pve", ev_vb.get(bank, 0))
            vt = b.tr(pb[bank][:, 0:128],
                      P["vp"][:, kt * NKEYP + kc * 128: kt * NKEYP + (kc + 1) * 128],
                      P["idn"])
            b.W("ve", "pte", vt)
            ev_vb[bank] = b.ve("tensor_copy",
                               P["vpt"][:, kc * 256 + kt * 128:
                                        kc * 256 + (kt + 1) * 128],
                               pb[bank][:, 0:128])
    if DEBUG:
        b.barrier()
        b.dma("ge", dbg["d_vpt"], P["vpt"])

    b.phase("PB")
    b.barrier()
    qhat = lambda m: Z0(m * 1024, 1024)
    wqbh = lambda s: Z0(8192 + s * 512, 512).bitcast(BF16)
    wqbl = lambda s: Z0(10240 + s * 512, 512).bitcast(BF16)
    xnch = lambda s: Z0(12288 + s * 256, 256).bitcast(BF16)
    xncl = lambda s: Z0(13312 + s * 256, 256).bitcast(BF16)
    rqA = Z1(0, 1024)
    rqB = Z1(1024, 1024)
    ropeqt = Z1(3072, 1024)
    b.dma("sy", rqA, ropeq_d[0], sem="s_rq")
    b.dma("sy", rqB, ropeq_d[1], sem="s_rq")
    v_rq = b.cnt["s_rq"]
    ev_slot = {}
    ev_qbank = 0
    for half in range(2):
        for c in range(16):
            s = c % 4
            b.W("ge", "pte", ev_slot.get(s, 0))
            b.dma("ge", wqbh(s), wq_bh[c], sem=f"s_w{s}")
            b.dma("ge", wqbl(s), wq_bl[c], sem=f"s_w{s}")
            vw = b.cnt[f"s_w{s}"]
            b.W("sy", "pte", ev_slot.get(s, 0))
            b.dma("sy", xnch(s), xn_dh[c, :, half * 512:(half + 1) * 512],
                  sem=f"s_q{s}")
            b.dma("sy", xncl(s), xn_dl[c, :, half * 512:(half + 1) * 512],
                  sem=f"s_q{s}")
            vx = b.cnt[f"s_q{s}"]
            b.W("te", f"s_w{s}", vw)
            b.W("te", f"s_q{s}", vx)
            if c == 0:
                b.W("te", "pve", ev_qbank)
            for m in range(8):
                wh = wqbh(s)[:, m * 128:(m + 1) * 128]
                wl = wqbl(s)[:, m * 128:(m + 1) * 128]
                b.mm(pb[m][:, 0:512], wh, xnch(s),
                     start=(c == 0), stop=False, inc=False)
                b.mm(pb[m][:, 0:512], wh, xncl(s),
                     start=False, stop=False, inc=False)
                b.mm(pb[m][:, 0:512], wl, xnch(s),
                     start=False, stop=(c == 15), inc=(m == 7))
            ev_slot[s] = b.cnt["pte"]
        v_q = b.cnt["pte"]
        b.W("ve", "pte", v_q)
        for m in range(8):
            b.ve("tensor_scalar_mul",
                 qhat(m)[:, half * 512:(half + 1) * 512], pb[m][:, 0:512],
                 P["smv"][:, m:m + 1])
        ev_qbank = b.cnt["pve"]
    b.W("ve", "s_rq", v_rq)
    for m in range(8):
        rope(qhat(m), qhat(m), rqA, rqB, ropeqt, 1024)
    if DEBUG:
        b.barrier()
        b.dma("ge", dbg["d_qhat"], Z0(0, 8192))

    b.phase("PC")
    b.barrier()
    maskb = lambda s: Z0(8192 + s * NKEYP, NKEYP)
    probs = Z0(10496, NKEYP)
    probsn = Z0(11648, NKEYP)
    probst = Z0(12800, NKEYP)
    red = lambda j: Z0(13952 + j, 1)
    ev_mask = {}
    ev_sc = 0
    ev_pt = 0
    ev_av = 0
    ev_probsn_rd = 0
    ev_probst_rd = 0
    for qt in range(NQT):
        ms = qt % 2
        b.W("sy", "pve", ev_mask.get(ms, 0))
        b.dma("sy", maskb(ms), mask_d[qt], sem=f"s_m{ms}")
        v_m = b.cnt[f"s_m{ms}"]
        for h in range(H):
            g = h // 4
            hh = h // 2
            lhs = qhat(hh)[(h % 2) * 64:(h % 2) * 64 + 64,
                           qt * 128:(qt + 1) * 128]
            kp0 = (g // 2) * NKEYP
            qb = (h % 2) * 64
            ksrc = P["kp"] if (h % 2) == (g % 2) else P["kps"]
            krows = ksrc[qb:qb + 64, :]
            b.W("te", "pve", ev_sc)
            b.mm(pb[0][:, 0:512], lhs, krows[:, kp0:kp0 + 512], inc=False)
            b.mm(pb[1][:, 0:512], lhs, krows[:, kp0 + 512:kp0 + 1024],
                 inc=False)
            v_s = b.mm(pb[2][:, 0:128], lhs, krows[:, kp0 + 1024:kp0 + 1152])
            b.W("ve", "pte", v_s)
            b.W("ve", f"s_m{ms}", v_m)
            b.ve("tensor_tensor", probs[:, 0:512], pb[0][:, 0:512],
                 maskb(ms)[:, 0:512], op=ALU.min)
            b.ve("tensor_tensor", probs[:, 512:1024], pb[1][:, 0:512],
                 maskb(ms)[:, 512:1024], op=ALU.min)
            ev_sc = b.ve("tensor_tensor", probs[:, 1024:1152], pb[2][:, 0:128],
                         maskb(ms)[:, 1024:1152], op=ALU.min)
            vred = b.ve("reduce_max", red(0), probs, axis=AX.X, negate=True)
            b.W("se", "pve", vred)
            vexp = b.se("activation", probs, probs, AF.Exp, bias=red(0),
                        accum_out=red(1))
            b.W("ve", "pse", vexp)
            b.ve("reciprocal", red(2), red(1))
            b.W("ve", "pte", ev_probsn_rd)
            vpn = b.ve("tensor_scalar_mul", probsn, probs, red(2))
            b.W("te", "pve", vpn)
            b.W("te", "pve", ev_pt)
            for kc in range(9):
                b.tr(pb[3 + kc // 4][:, (kc % 4) * 128:(kc % 4) * 128 + 128],
                     probsn[:, kc * 128:(kc + 1) * 128], P["idn"])
            ev_probsn_rd = b.cnt["pte"]
            v_t = b.cnt["pte"]
            b.W("ve", "pte", v_t)
            b.W("ve", "pte", ev_probst_rd)
            b.ve("tensor_copy", probst[:, 0:512], pb[3][:, 0:512])
            b.ve("tensor_copy", probst[:, 512:1024], pb[4][:, 0:512])
            ev_pt = b.ve("tensor_copy", probst[:, 1024:1152], pb[5][:, 0:128])
            b.W("te", "pve", ev_pt)
            b.W("te", "pve", ev_av)
            for kc in range(9):
                b.mm(pb[6][0:64, 0:128],
                     P["vpt"][:, kc * 256 + g * 64: kc * 256 + g * 64 + 64],
                     probst[:, kc * 128:(kc + 1) * 128],
                     start=(kc == 0), stop=(kc == 8), inc=(kc == 8))
            v_av = b.cnt["pte"]
            ev_probst_rd = v_av
            b.W("ve", "pte", v_av)
            ev_av = b.ve("tensor_copy",
                         P["attnv"][(h % 2) * 64:(h % 2) * 64 + 64,
                                    hh * TO + qt * 128: hh * TO + (qt + 1) * 128],
                         pb[6][0:64, 0:128])
        ev_mask[ms] = ev_sc
    if DEBUG:
        b.barrier()
        b.dma("ge", dbg["d_attnv"], P["attnv"])

    b.phase("PE1")
    b.barrier()
    x1h = Z0(0, 8192).bitcast(BF16)
    x1l = Z0(8192, 8192).bitcast(BF16)
    zba = lambda a: Z1(a * 256, 256)
    zbb = lambda a: Z1(4096 + a * 256, 256)
    xn1h = lambda s: Z1(8192 + s * 128, 128).bitcast(BF16)
    xn1l = lambda s: Z1(8704 + s * 128, 128).bitcast(BF16)
    ev_x1bank = {}
    ev_xnc1 = {}
    ev_x1c = 0
    for tc in range(4):
        for a in range(16):
            s = a % 4
            b.W("sy", "pte", ev_xnc1.get(s, 0))
            b.dma("sy", xn1h(s), xn_dh[a, :, tc * 256:(tc + 1) * 256],
                  sem=f"s_e{s}")
            b.dma("sy", xn1l(s), xn_dl[a, :, tc * 256:(tc + 1) * 256],
                  sem=f"s_e{s}")
            vx = b.cnt[f"s_e{s}"]
            b.W("te", f"s_e{s}", vx)
            bank = a % 4
            b.W("te", "pve", ev_x1bank.get(bank, 0))
            hs = P["hsu_ug"][:, a * 128:(a + 1) * 128]
            b.mm(pb[bank][:, 0:256], hs, xn1h(s), start=True, stop=False,
                 inc=False)
            v_m = b.mm(pb[bank][:, 0:256], hs, xn1l(s), start=False, stop=True)
            ev_xnc1[s] = v_m
            b.W("ve", "pte", v_m)
            ev_x1bank[bank] = b.ve("tensor_copy", zba(a), pb[bank][:, 0:256])
        for stage, hbit in enumerate([1, 2, 4, 8]):
            src = zba if stage % 2 == 0 else zbb
            for i in range(16):
                jlo, jhi = i & ~hbit, i | hbit
                op = ALU.add if (i & hbit) == 0 else ALU.subtract
                dst = (zbb if stage % 2 == 0 else zba)(i)
                b.ve("tensor_tensor", dst, src(jlo), src(jhi), op=op)
        v_bf = b.cnt["pve"]
        b.W("se", "pve", v_bf)
        for i in range(16):
            hslice = x1h[:, i * 1024 + tc * 256: i * 1024 + tc * 256 + 256]
            lslice = x1l[:, i * 1024 + tc * 256: i * 1024 + tc * 256 + 256]
            vhi = b.se("copy", hslice, zba(i))
            b.W("ve", "pse", vhi)
            ev_x1c = b.ve("tensor_tensor", lslice, zba(i), hslice,
                          op=ALU.subtract)
    if DEBUG:
        b.barrier()
        b.dma("ge", dbg["d_x1"], Z0(0, 16384))
    if DEBUG2:
        b.barrier()
        b.dma("ge", dbg2["e_x1"], Z0(0, 16384))

    b.phase("P1")
    b.barrier()
    wubh = lambda s: Z1(s * 1024, 1024).bitcast(BF16)
    wubl = lambda s: Z1(2048 + s * 1024, 1024).bitcast(BF16)
    wgbh = lambda s: Z1(4096 + s * 1024, 1024).bitcast(BF16)
    wgbl = lambda s: Z1(6144 + s * 1024, 1024).bitcast(BF16)
    hsud = lambda s: Z1(8192 + s * 64, 64).bitcast(BF16)
    gsil = Z1(8448, 1024)
    gtile = Z1(9472, 1024)
    gth = Z1(10496, 512).bitcast(BF16)
    gtl = Z1(11008, 512).bitcast(BF16)
    ytile = lambda s: Z1(11520 + s * 1024, 1024)
    ev_w = {}
    ev_hsu = {}
    ev_ybank = 0
    ev_ysb = {}
    ev_upbank = 0
    ev_gdone = 0
    ev_stageA = 0
    for a in range(64):
        s = a % 2
        b.W("sy", "pte", ev_w.get(s, 0))
        b.dma("sy", wubh(s), wup_bh[a], sem=f"s_u{s}")
        b.dma("sy", wubl(s), wup_bl[a], sem=f"s_u{s}")
        b.dma("sy", wgbh(s), wg_bh[a], sem=f"s_u{s}")
        b.dma("sy", wgbl(s), wg_bl[a], sem=f"s_u{s}")
        vw = b.cnt[f"s_u{s}"]
        b.W("ve", "pte", ev_hsu.get(s, 0))
        vh = b.ve("tensor_scalar_mul", hsud(s), P["hcb"], P["sud"][:, a:a + 1])
        b.W("te", f"s_u{s}", vw)
        b.W("te", "pve", vh)
        b.W("te", "pse", ev_upbank)
        b.W("te", "pve", ev_gdone)
        for c in range(16):
            for half in range(2):
                xh = x1h[:, c * 1024 + half * 512: c * 1024 + (half + 1) * 512]
                xl = x1l[:, c * 1024 + half * 512: c * 1024 + (half + 1) * 512]
                cs = slice(c * 128, (c + 1) * 128)
                b.mm(pb[half][:, 0:512], wubh(s)[:, cs], xh,
                     start=(c == 0), stop=False, inc=False)
                b.mm(pb[half][:, 0:512], wubh(s)[:, cs], xl,
                     start=False, stop=False, inc=False)
                b.mm(pb[half][:, 0:512], wubl(s)[:, cs], xh,
                     start=False, stop=(c == 15), inc=False)
                b.mm(pb[2 + half][:, 0:512], wgbh(s)[:, cs], xh,
                     start=(c == 0), stop=False, inc=False)
                b.mm(pb[2 + half][:, 0:512], wgbh(s)[:, cs], xl,
                     start=False, stop=False, inc=False)
                b.mm(pb[2 + half][:, 0:512], wgbl(s)[:, cs], xh,
                     start=False, stop=(c == 15),
                     inc=(c == 15 and half == 1))
        ev_w[s] = b.cnt["pte"]
        v_up = b.cnt["pte"]
        b.W("se", "pte", v_up)
        for half in range(2):
            b.se("activation", gsil[:, half * 512:(half + 1) * 512],
                 pb[2 + half][:, 0:512], AF.Silu,
                 scale=P["sug"][:, 64 + a:64 + a + 1])
        ev_upbank = b.cnt["pse"]
        v_sil = b.cnt["pse"]
        b.W("ve", "pse", v_sil)
        for half in range(2):
            b.ve("scalar_tensor_tensor", gtile[:, half * 512:(half + 1) * 512],
                 pb[half][:, 0:512], P["sug"][:, a:a + 1],
                 gsil[:, half * 512:(half + 1) * 512],
                 op0=ALU.mult, op1=ALU.mult)
        vgd = b.cnt["pve"]
        b.W("se", "pve", vgd)
        b.W("se", "pte", ev_stageA)
        vghi = b.se("copy", gth, gtile)
        b.W("ve", "pse", vghi)
        ev_gdone = b.ve("tensor_tensor", gtl, gtile, gth, op=ALU.subtract)
        b.W("te", "pve", ev_gdone)
        b.W("te", "pve", ev_ybank)
        for half in range(2):
            hslc = slice(half * 512, (half + 1) * 512)
            b.mm(pb[4 + half][:, 0:512], hsud(s), gth[:, hslc],
                 start=True, stop=False, inc=False)
            b.mm(pb[4 + half][:, 0:512], hsud(s), gtl[:, hslc],
                 start=False, stop=True, inc=(half == 1))
        ev_hsu[s] = b.cnt["pte"]
        ev_stageA = b.cnt["pte"]
        v_y = b.cnt["pte"]
        b.W("ve", "pte", v_y)
        ys = a % 2
        b.W("ve", f"s_y{ys}", ev_ysb.get(ys, 0))
        b.ve("tensor_copy", ytile(ys)[:, 0:512], pb[4][:, 0:512])
        vy2 = b.ve("tensor_copy", ytile(ys)[:, 512:1024], pb[5][:, 0:512])
        ev_ybank = vy2
        b.W("ge", "pve", vy2)
        b.dma("ge", y_d[a], ytile(ys), sem=f"s_y{ys}")
        ev_ysb[ys] = b.cnt[f"s_y{ys}"]
    if DEBUG:
        b.barrier()
        b.dma("ge", dbg["d_y"], y_d)

    b.phase("P2")
    b.barrier()
    p2a = lambda a: Z0(a * 256, 256)
    p2b = lambda a: Z1(a * 256, 256)
    x2hb = lambda a: Z1(a * 128, 128).bitcast(BF16)
    x2lb = lambda a: Z1(8192 + a * 128, 128).bitcast(BF16)
    v_wr = 0
    v_cv_se = 0
    v_cv_ve = 0
    for tc in range(4):
        if tc > 0:
            b.W("sy", "pse", v_cv_se)
            b.W("sy", "pve", v_cv_ve)
            b.W("ge", "pse", v_cv_se)
            b.W("ge", "pve", v_cv_ve)
            b.W("sy", "dge", v_wr)
            b.W("ge", "dge", v_wr)
        for a in range(64):
            b.dma("sy" if a % 2 == 0 else "ge", p2a(a),
                  y_d[a, :, tc * 256:(tc + 1) * 256])
        v_ld_sy = b.cnt["dsy"]
        v_ld_ge = b.cnt["dge"]
        b.W("ve", "dsy", v_ld_sy)
        b.W("ve", "dge", v_ld_ge)
        for stage, hbit in enumerate([1, 2, 4, 8, 16, 32]):
            src = p2a if stage % 2 == 0 else p2b
            dst = p2b if stage % 2 == 0 else p2a
            for i in range(64):
                jlo, jhi = i & ~hbit, i | hbit
                op = ALU.add if (i & hbit) == 0 else ALU.subtract
                b.ve("tensor_tensor", dst(i), src(jlo), src(jhi), op=op)
        v_bf = b.cnt["pve"]
        b.W("se", "pve", v_bf)
        if tc > 0:
            b.W("se", "dge", v_wr)
        for a in range(64):
            vhi = b.se("copy", x2hb(a), p2a(a))
            b.W("ve", "pse", vhi)
            b.ve("tensor_tensor", x2lb(a), p2a(a), x2hb(a), op=ALU.subtract)
        v_cv_se = b.cnt["pse"]
        v_cv_ve = b.cnt["pve"]
        b.W("ge", "pse", v_cv_se)
        b.W("ge", "pve", v_cv_ve)
        for a in range(64):
            b.dma("ge", x2_dh[a, :, tc * 256:(tc + 1) * 256], x2hb(a))
            b.dma("ge", x2_dl[a, :, tc * 256:(tc + 1) * 256], x2lb(a))
        v_wr = b.cnt["dge"]
    if DEBUG:
        b.barrier()
        b.dma("ge", dbg["d_x2"], x2_dh)

    b.phase("P3")
    b.barrier()
    acc = lambda m: Z0(m * 1024, 1024)
    wdbh = lambda s: Z1(s * 256, 256).bitcast(BF16)
    wdbl = lambda s: Z1(1024 + s * 256, 256).bitcast(BF16)
    x2ch = lambda s: Z1(2048 + s * 512, 512).bitcast(BF16)
    x2cl = lambda s: Z1(4096 + s * 512, 512).bitcast(BF16)
    ev_slot3 = {}
    ev_p3bank = 0
    for sw in range(4):
        for k in range(64):
            s = k % 4
            b.W("ge", "pte", ev_slot3.get(s, 0))
            b.dma("ge", wdbh(s), wd_bh[sw, k], sem=f"s_d{s}")
            b.dma("ge", wdbl(s), wd_bl[sw, k], sem=f"s_d{s}")
            vw = b.cnt[f"s_d{s}"]
            b.W("sy", "pte", ev_slot3.get(s, 0))
            b.dma("sy", x2ch(s), x2_dh[k], sem=f"s_x{s}")
            b.dma("sy", x2cl(s), x2_dl[k], sem=f"s_x{s}")
            vx = b.cnt[f"s_x{s}"]
            b.W("te", f"s_d{s}", vw)
            b.W("te", f"s_x{s}", vx)
            if k == 0:
                b.W("te", "pve", ev_p3bank)
            for j in range(4):
                for half in range(2):
                    js = slice(j * 128, (j + 1) * 128)
                    hs = slice(half * 512, (half + 1) * 512)
                    b.mm(pb[j * 2 + half][:, 0:512], wdbh(s)[:, js],
                         x2ch(s)[:, hs], start=(k == 0), stop=False, inc=False)
                    b.mm(pb[j * 2 + half][:, 0:512], wdbh(s)[:, js],
                         x2cl(s)[:, hs], start=False, stop=False, inc=False)
                    b.mm(pb[j * 2 + half][:, 0:512], wdbl(s)[:, js],
                         x2ch(s)[:, hs], start=False, stop=(k == 63),
                         inc=(j == 3 and half == 1))
            ev_slot3[s] = b.cnt["pte"]
        v_p3 = b.cnt["pte"]
        b.W("ve", "pte", v_p3)
        for j in range(4):
            for half in range(2):
                b.ve("tensor_scalar_mul",
                     acc(sw * 4 + j)[:, half * 512:(half + 1) * 512],
                     pb[j * 2 + half][:, 0:512],
                     P["sdp"][:, sw * 4 + j:sw * 4 + j + 1])
        ev_p3bank = b.cnt["pve"]

    b.phase("PD")
    b.barrier()
    wobh = lambda s: Z1(6144 + s * 512, 512).bitcast(BF16)
    wobl = lambda s: Z1(7168 + s * 512, 512).bitcast(BF16)
    avh = Z1(8192, 4096).bitcast(BF16)
    avl = Z1(12288, 4096).bitcast(BF16)
    vhi = b.se("copy", avh, P["attnv"])
    b.W("ve", "pse", vhi)
    v_avc = b.ve("tensor_tensor", avl, P["attnv"], avh, op=ALU.subtract)
    b.W("te", "pve", v_avc)
    ev_wo = {}
    ev_wobank = 0
    for m in range(16):
        s = m % 2
        b.W("ge", "pte", ev_wo.get(s, 0))
        b.dma("ge", wobh(s), wo_bh[m], sem=f"s_o{s}")
        b.dma("ge", wobl(s), wo_bl[m], sem=f"s_o{s}")
        vw = b.cnt[f"s_o{s}"]
        b.W("te", f"s_o{s}", vw)
        b.W("te", "pve", ev_wobank)
        for c in range(8):
            for half in range(2):
                cs = slice(c * 128, (c + 1) * 128)
                asl = slice(c * TO + half * 512, c * TO + (half + 1) * 512)
                b.mm(pb[half][:, 0:512], wobh(s)[:, cs], avh[:, asl],
                     start=(c == 0), stop=False, inc=False)
                b.mm(pb[half][:, 0:512], wobh(s)[:, cs], avl[:, asl],
                     start=False, stop=False, inc=False)
                b.mm(pb[half][:, 0:512], wobl(s)[:, cs], avh[:, asl],
                     start=False, stop=(c == 7),
                     inc=(c == 7 and half == 1))
        ev_wo[s] = b.cnt["pte"]
        v_wo = b.cnt["pte"]
        b.W("ve", "pte", v_wo)
        for half in range(2):
            b.ve("tensor_tensor", acc(m)[:, half * 512:(half + 1) * 512],
                 acc(m)[:, half * 512:(half + 1) * 512], pb[half][:, 0:512],
                 op=ALU.add)
        ev_wobank = b.cnt["pve"]
    if DEBUG:
        b.barrier()
        b.dma("ge", dbg["d_acc"], Z0(0, 16384))

    b.phase("PF")
    b.barrier()
    xob = lambda s: Z1(8192 + s * 2048, 2048)
    outb = lambda s: Z1(12288 + s * 2048, 2048)
    ev_xo = {}
    ev_outb = {}
    ev_fbank = {}
    for tt in range(NQT):
        s = tt % 2
        b.W("sy", "pve", ev_xo.get(s, 0))
        b.dma("sy", xob(s), xo[tt * 128:(tt + 1) * 128, :], sem=f"s_f{s}")
        v_x = b.cnt[f"s_f{s}"]
        for cg in range(4):
            bank = cg % 2
            b.W("te", "pve", ev_fbank.get(bank, 0))
            for cc in range(4):
                c = cg * 4 + cc
                b.tr(pb[bank][:, cc * 128:(cc + 1) * 128],
                     acc(c)[:, tt * 128:(tt + 1) * 128], P["idn"])
            v_t = b.cnt["pte"]
            b.W("ve", "pte", v_t)
            b.W("ve", f"s_f{s}", v_x)
            if cg == 0:
                b.W("ve", f"s_b{s}", ev_outb.get(s, 0))
            vv = b.ve("tensor_tensor", outb(s)[:, cg * 512:(cg + 1) * 512],
                      xob(s)[:, cg * 512:(cg + 1) * 512], pb[bank][:, 0:512],
                      op=ALU.add)
            ev_fbank[bank] = vv
        ev_xo[s] = b.cnt["pve"]
        b.W("ge", "pve", ev_xo[s])
        b.dma("ge", out_d[tt * 128:(tt + 1) * 128, :], outb(s), sem=f"s_b{s}")
        ev_outb[s] = b.cnt[f"s_b{s}"]
    if DEBUG2:
        b.barrier()
        b.dma("ge", dbg2["e_attnv"], P["attnv"])
        b.dma("ge", dbg2["e_y"], y_d)
        b.dma("ge", dbg2["e_x2"], x2_d)
        b.dma("ge", dbg2["e_xn"], xn_d)
    for s_, v_ in list(b.cnt.items()):
        if s_.startswith(("d", "s_")) and v_ > 0:
            b.W("ge", s_, v_)

    b.finish()
    return nc


def _host_prep(inputs):
    f32 = np.float32
    x = np.ascontiguousarray(np.asarray(inputs["L0_attn_x"], f32).reshape(T, D))
    WqS = np.asarray(inputs["L0_attn_Wq_sem"], f32)
    WkS = np.asarray(inputs["L0_attn_Wk_sem"], f32)
    WqG = np.asarray(inputs["L0_attn_Wq_geo"], f32)
    WkG = np.asarray(inputs["L0_attn_Wk_geo"], f32)
    Wv = np.asarray(inputs["L0_attn_Wv"], f32)
    Wo = np.asarray(inputs["L0_attn_Wo"], f32)
    ls = np.asarray(inputs["L0_attn_logit_scale"], f32)
    SUu = np.asarray(inputs["L0_ffn_SU_ug"], f32)
    SUd = np.asarray(inputs["L0_ffn_SU_dp"], f32)
    WSu = np.asarray(inputs["L0_ffn_Wscale_ug"], f32)
    WSd = np.asarray(inputs["L0_ffn_Wscale_dp"], f32)
    Wup = np.asarray(inputs["L0_ffn_W_up"], f32)
    Wg = np.asarray(inputs["L0_ffn_W_gate"], f32)
    Wd = np.asarray(inputs["L0_ffn_W_down"], f32)

    Wq_hat = np.empty((D, 1024), f32)
    for h in range(H):
        Wq_hat[:, h * 64:h * 64 + 32] = WqS[:, h * 32:(h + 1) * 32]
        Wq_hat[:, h * 64 + 32:h * 64 + 64] = WqG[:, h * 32:(h + 1) * 32]
    Wk_hat = np.empty((D, 256), f32)
    for g in range(HKV):
        Wk_hat[:, g * 64:g * 64 + 32] = WkS[:, g * 32:(g + 1) * 32]
        Wk_hat[:, g * 64 + 32:g * 64 + 64] = WkG[:, g * 32:(g + 1) * 32]
    Wkv_hat = np.concatenate([Wk_hat, Wv], axis=1)

    wq = np.ascontiguousarray(Wq_hat.reshape(16, 128, 1024))
    wkv = np.ascontiguousarray(
        Wkv_hat.reshape(16, 128, 4, 128).transpose(1, 0, 2, 3).reshape(128, 8192))
    wup = np.ascontiguousarray(
        Wup.T.reshape(16, 128, 64, 128).transpose(2, 1, 0, 3).reshape(64, 128, 2048))
    wgate = np.ascontiguousarray(
        Wg.T.reshape(16, 128, 64, 128).transpose(2, 1, 0, 3).reshape(64, 128, 2048))
    wdown = np.ascontiguousarray(
        Wd.T.reshape(64, 128, 4, 512).transpose(2, 0, 1, 3))
    wo = np.ascontiguousarray(
        Wo.reshape(8, 128, 16, 128).transpose(2, 1, 0, 3).reshape(16, 128, 1024))

    inv = (10000.0 ** -(np.arange(16) / 16.0)).astype(np.float64)
    ang = np.arange(T, dtype=np.float64)[:, None] * inv[None, :]
    cosT, sinT = np.cos(ang).astype(f32), np.sin(ang).astype(f32)
    A = np.zeros((128, T), f32)
    Btab = np.zeros((128, T), f32)
    for blk in range(2):
        r0 = blk * 64
        A[r0:r0 + 32, :] = 1.0
        for i in range(16):
            A[r0 + 32 + i, :] = cosT[:, i]
            A[r0 + 48 + i, :] = cosT[:, i]
            Btab[r0 + 32 + i, :] = -sinT[:, i]
            Btab[r0 + 48 + i, :] = sinT[:, i]
    ropek = np.ascontiguousarray(
        np.stack([A, Btab], 0).reshape(2, 128, NHG, HG).transpose(2, 0, 1, 3))

    k_pos = np.concatenate([np.arange(NBLK) * MBLK + (MBLK - 1),
                            np.arange(REMOTE, T),
                            np.full(NKEYP - NKEY, 2 * T)]).astype(np.int64)
    ispad = np.arange(NKEYP) >= NKEY

    in_maps = []
    for c in range(NCORE):
        q = np.arange(c * TO, (c + 1) * TO)
        vis = k_pos[None, :] <= q[:, None]
        cap = np.where(vis, POS,
                       np.where(ispad[None, :], NEGPAD, NEG)).astype(f32)
        im = {
            "xf": x, "xo": np.ascontiguousarray(x[c * TO:(c + 1) * TO]),
            "wq": wq, "wkv": wkv, "wup": wup, "wgate": wgate,
            "wdown": wdown, "wo": wo,
            "su_ug": np.ascontiguousarray(SUu.reshape(16, 128)),
            "su_dp": np.ascontiguousarray(SUd.reshape(64, 128)),
            "wsc_ug": np.ascontiguousarray(WSu.reshape(128, 128)),
            "wsc_dp": np.ascontiguousarray(WSd.reshape(16, 128)),
            "ls": np.ascontiguousarray(ls.reshape(1, 16)),
            "mask": np.ascontiguousarray(cap.reshape(NQT, 128, NKEYP)),
            "ropeq": np.ascontiguousarray(
                np.stack([A, Btab], 0)[:, :, c * TO:(c + 1) * TO]),
            "ropek": ropek,
        }
        in_maps.append(im)
    return in_maps


def kernel(**inputs):
    in_maps = _host_prep(inputs)
    nc = build_program()
    res = run_bass_kernel_spmd(nc, in_maps, core_ids=list(range(NCORE)))
    out = np.concatenate([r["out"] for r in res.results], axis=0)
    return out.reshape(1, T, D).astype(np.float32)


# version 04
# speedup vs baseline: 1.0924x; 1.0264x over previous
import numpy as np

import concourse.bass as bass
import concourse.mybir as mybir
from concourse.bass_utils import run_bass_kernel_spmd

F32 = mybir.dt.float32
BF16 = mybir.dt.bfloat16
AF = mybir.ActivationFunctionType
ALU = mybir.AluOpType
AX = mybir.AxisListType

NCORE = 8
T, D, INTER = 8192, 2048, 8192
TO = T // NCORE
H, HKV, DV = 16, 4, 64
REMOTE, MBLK, LW = 7168, 64, 1024
NBLK = REMOTE // MBLK
NKEY = NBLK + LW
NKEYP = 1152
NEG, NEGPAD, POS = -1.0e9, -2.0e9, 1.0e30
HG = 256
NHG = T // HG
NOG = TO // HG
NQT = TO // 128
EPS = 1e-5

DEBUG = False
DEBUG2 = False
PE1_BARRIER = False
PHASES = None
ENGINES = ("te", "ve", "se", "ge", "sy")
SHUF16 = [(i + 16) % 32 for i in range(32)]


def _had(n):
    h = np.array([[1.0]])
    while h.shape[0] < n:
        h = np.block([[h, h], [h, -h]])
    return h


class B:
    def __init__(self, nc):
        self.nc = nc
        self.ops = {e: [] for e in ENGINES}
        self.cnt = {}
        self.sems = {}
        self.enabled = True


    def phase(self, name):
        self.enabled = (PHASES is None) or (name in PHASES)

    def sem(self, name):
        if name not in self.sems:
            self.sems[name] = self.nc.alloc_semaphore(name)
            self.cnt[name] = 0
        return self.sems[name]

    def W(self, eng, sem, val):
        if not self.enabled:
            return
        if not val or val <= 0:
            return
        self.sem(sem)
        self.ops[eng].append(("w", sem, val))

    def _emit(self, eng, meth, args, kw, inc):
        if not self.enabled:
            return self.cnt.get(inc[0], 0) if inc is not None else None
        if inc is not None:
            self.sem(inc[0])
            self.cnt[inc[0]] += inc[1]
        self.ops[eng].append(("e", meth, args, kw, inc))
        return self.cnt[inc[0]] if inc is not None else None

    def ve(self, meth, *a, **kw):
        self.W("ve", "pve", self.cnt.get("pve", 0))
        return self._emit("ve", meth, a, kw, ("pve", 1))

    def se(self, meth, *a, **kw):
        self.W("se", "pse", self.cnt.get("pse", 0))
        return self._emit("se", meth, a, kw, ("pse", 1))

    def ge(self, meth, *a, **kw):
        self.W("ge", "pge", self.cnt.get("pge", 0))
        return self._emit("ge", meth, a, kw, ("pge", 1))

    def mm(self, out, lhsT, rhs, start=True, stop=True, inc=True, **kw):
        return self._emit("te", "matmul", (out, lhsT, rhs),
                          dict(start=start, stop=stop, **kw),
                          ("pte", 1) if inc else None)

    def tr(self, out, in_, ident):
        return self._emit("te", "matmul", (out, in_, ident),
                          dict(is_transpose=True), ("pte", 1))

    def dma(self, q, out, in_, sem=None):
        return self._emit(q, "dma_start", (), dict(out=out, in_=in_),
                          (sem or ("d" + q), 16))

    def barrier(self):
        snap = [(s, v) for s, v in self.cnt.items() if v > 0]
        for e in ENGINES:
            for s, v in snap:
                self.W(e, s, v)

    def finish(self):
        with self.nc.Block() as block:
            regs = {"sy": block.sync, "te": block.tensor, "ve": block.vector,
                    "se": block.scalar, "ge": block.gpsimd}
            for name in ENGINES:
                lst = self.ops[name]

                def body(E, lst=lst):
                    for op in lst:
                        if op[0] == "w":
                            E.wait_ge(self.sems[op[1]], op[2])
                        else:
                            _, meth, args, kw, inc = op
                            i = getattr(E, meth)(*args, **kw)
                            if inc is not None:
                                i.then_inc(self.sems[inc[0]], inc[1])
                regs[name](body)


def build_program():
    nc = bass.Bass("TRN2", target_bir_lowering=False, debug=False,
                   detect_race_conditions=False)
    b = B(nc)
    for s_ in ("pte", "pve", "pse", "pge", "dsy", "dge", "dte", "dve", "dse"):
        b.sem(s_)

    din = lambda n, s: nc.dram_tensor(n, s, F32, kind="ExternalInput").ap()
    xf = din("xf", [T, D])
    xo = din("xo", [TO, D])
    wq_d = din("wq", [16, 128, 1024])
    wkv_d = din("wkv", [128, 8192])
    wup_d = din("wup", [64, 128, 2048])
    wgate_d = din("wgate", [64, 128, 2048])
    wdown_d = din("wdown", [4, 64, 128, 512])
    wo_d = din("wo", [16, 128, 1024])
    suu_d = din("su_ug", [16, 128])
    sud_d = din("su_dp", [64, 128])
    sug_d = din("wsc_ug", [128, 128])
    sdp_d = din("wsc_dp", [16, 128])
    ls_d = din("ls", [1, 16])
    mask_d = din("mask", [NQT, 128, NKEYP])
    ropeq_d = din("ropeq", [2, 128, TO])
    ropek_d = din("ropek", [NHG, 2, 128, HG])
    out_d = nc.dram_tensor("out", [TO, D], F32, kind="ExternalOutput").ap()

    hc_d = nc.inline_tensor((_had(128) / 64.0).astype(np.float32),
                            name="hado64").ap()
    id_d = nc.inline_tensor(np.eye(128, dtype=np.float32), name="identc").ap()
    ones_d = nc.inline_tensor(np.ones((1, 128), dtype=np.float32),
                              name="onesr").ap()
    cv_d = nc.inline_tensor(np.array([[32.0, 2048.0, 8192.0, 1.0]],
                                     dtype=np.float32), name="cvals").ap()
    eps_d = nc.inline_tensor(np.full((1, 1), EPS, dtype=np.float32),
                             name="epsc").ap()
    import ml_dtypes
    hcb_d = nc.inline_tensor((_had(128) / 64.0).astype(ml_dtypes.bfloat16),
                             name="hadb").ap()

    xn_dh = nc.dram_tensor("xn_sh", [16, 128, TO], BF16, kind="Internal").ap()
    xn_dl = nc.dram_tensor("xn_sl", [16, 128, TO], BF16, kind="Internal").ap()
    y_d = nc.dram_tensor("y_s", [64, 128, TO], F32, kind="Internal").ap()
    x2_dh = nc.dram_tensor("x2_sh", [64, 128, TO], BF16, kind="Internal").ap()
    x2_dl = nc.dram_tensor("x2_sl", [64, 128, TO], BF16, kind="Internal").ap()
    wq_bh = nc.dram_tensor("wq_bh", [16, 128, 1024], BF16, kind="Internal").ap()
    wq_bl = nc.dram_tensor("wq_bl", [16, 128, 1024], BF16, kind="Internal").ap()
    wo_bh = nc.dram_tensor("wo_bh", [16, 128, 1024], BF16, kind="Internal").ap()
    wo_bl = nc.dram_tensor("wo_bl", [16, 128, 1024], BF16, kind="Internal").ap()
    wup_bh = nc.dram_tensor("wup_bh", [64, 128, 2048], BF16, kind="Internal").ap()
    wup_bl = nc.dram_tensor("wup_bl", [64, 128, 2048], BF16, kind="Internal").ap()
    wg_bh = nc.dram_tensor("wg_bh", [64, 128, 2048], BF16, kind="Internal").ap()
    wg_bl = nc.dram_tensor("wg_bl", [64, 128, 2048], BF16, kind="Internal").ap()
    wd_bh = nc.dram_tensor("wd_bh", [4, 64, 128, 512], BF16, kind="Internal").ap()
    wd_bl = nc.dram_tensor("wd_bl", [4, 64, 128, 512], BF16, kind="Internal").ap()

    perm = {}
    for name, shape in [
            ("hc", [128, 128]), ("idn", [128, 128]), ("ones", [1, 128]),
            ("cv", [1, 4]), ("rcv", [1, 4]), ("rcvb", [128, 4]),
            ("suu", [128, 16]), ("sud", [128, 64]),
            ("sug", [128, 128]), ("sdp", [128, 16]),
            ("lse", [1, 16]), ("qsc", [128, 16]), ("smv", [128, 8]),
            ("epsb", [128, 1]), ("eps1", [1, 1]),

            ("kp", [128, 2 * NKEYP]), ("kps", [128, 2 * NKEYP]),
            ("vp", [128, 2 * NKEYP]),
            ("vpt", [128, 9 * 256]),
            ("attnv", [128, 8 * TO]),
    ]:
        perm[name] = nc.alloc_sbuf_tensor(name, shape, F32)
    perm["hcb"] = nc.alloc_sbuf_tensor("hcb", [128, 128], BF16)
    perm["hsu_ug"] = nc.alloc_sbuf_tensor("hsu_ug", [128, 2048], BF16)
    z0 = nc.alloc_sbuf_tensor("z0", [128, 16384], F32)
    z1 = nc.alloc_sbuf_tensor("z1", [128, 16384], F32)
    Z0 = lambda off, n, p=128: z0[0:p, off:off + n]
    Z1 = lambda off, n, p=128: z1[0:p, off:off + n]
    P = {k: v[:] for k, v in perm.items()}

    pb = [nc.alloc_psum_tensor(f"pb{i}", [128, 512], F32) for i in range(8)]

    dbg2 = {}
    if DEBUG2:
        for name, shape in [("e_attnv", [128, 8 * TO]), ("e_y", [64, 128, TO]),
                            ("e_x2", [64, 128, TO]), ("e_xn", [16, 128, TO]),
                            ("e_x1", [128, 16384])]:
            dbg2[name] = nc.dram_tensor(name, shape, F32,
                                        kind="ExternalOutput").ap()
    dbg = {}
    if DEBUG:
        for name, shape in [("d_kp", [128, 2 * NKEYP]), ("d_vp", [128, 2 * NKEYP]),
                            ("d_qhat", [128, 8192]), ("d_attnv", [128, 8 * TO]),
                            ("d_x1", [128, 16384]), ("d_acc", [128, 16384]),
                            ("d_xn", [16, 128, TO]), ("d_vpt", [128, 9 * 256]),
                            ("d_y", [64, 128, TO]), ("d_x2", [64, 128, TO])]:
            dbg[name] = nc.dram_tensor(name, shape, F32,
                                       kind="ExternalOutput").ap()

    b.phase("P0")
    b.dma("sy", P["hc"], hc_d)
    b.dma("sy", P["idn"], id_d)
    b.dma("sy", P["ones"], ones_d)
    b.dma("sy", P["cv"], cv_d)
    b.dma("sy", P["eps1"], eps_d)
    b.dma("sy", P["lse"], ls_d)
    st_suu = Z1(4608, 128, 16)
    st_sud = Z1(4736, 128, 64)
    st_sug = Z1(4864, 128, 128)
    st_sdp = Z1(4992, 128, 16)
    b.dma("sy", st_suu, suu_d)
    b.dma("sy", st_sud, sud_d)
    b.dma("sy", st_sug, sug_d)
    b.dma("sy", st_sdp, sdp_d)
    b.dma("sy", P["hcb"], hcb_d)
    b.dma("sy", Z0(0, 8192), wkv_d)
    v_c = b.cnt["dsy"]

    b.W("se", "dsy", v_c)
    b.se("activation", P["rcv"], P["cv"], AF.Sqrt)
    vse = b.se("activation", P["lse"], P["lse"], AF.Exp)
    b.W("ve", "pse", vse)
    b.ve("reciprocal", P["rcv"], P["rcv"])
    vve = b.ve("tensor_scalar_mul", P["lse"], P["lse"], P["rcv"][0:1, 0:1])
    b.W("te", "pve", vve)
    b.W("te", "dsy", v_c)
    b.mm(pb[0][:, 0:4], P["ones"], P["rcv"], inc=False)
    b.mm(pb[0][:, 8:9], P["ones"], P["eps1"], inc=False)
    b.mm(pb[0][:, 16:32], P["ones"], P["lse"], inc=False)
    b.tr(pb[1][:, 0:16], st_suu, P["idn"][0:16, 0:16])
    b.tr(pb[1][:, 16:80], st_sud, P["idn"][0:64, 0:64])
    b.tr(pb[2][:, 0:128], st_sug, P["idn"])
    vte = b.tr(pb[1][:, 80:96], st_sdp, P["idn"][0:16, 0:16])
    b.W("ve", "pte", vte)
    b.ve("tensor_copy", P["rcvb"], pb[0][:, 0:4])
    b.ve("tensor_copy", P["epsb"], pb[0][:, 8:9])
    b.ve("tensor_copy", P["qsc"], pb[0][:, 16:32])
    b.ve("tensor_copy", P["suu"], pb[1][:, 0:16])
    b.ve("tensor_copy", P["sud"], pb[1][:, 16:80])
    b.ve("tensor_copy", P["sdp"], pb[1][:, 80:96])
    b.ve("tensor_copy", P["sug"], pb[2][:, 0:128])
    for m in range(8):
        b.ve("tensor_copy", P["smv"][0:64, m:m + 1],
             P["qsc"][0:64, 2 * m:2 * m + 1])
        b.ve("tensor_copy", P["smv"][64:128, m:m + 1],
             P["qsc"][64:128, 2 * m + 1:2 * m + 2])
    b.ve("tensor_scalar_mul", P["sug"], P["sug"], 64.0)
    b.ve("tensor_scalar_mul", P["sug"], P["sug"], P["rcvb"][:, 1:2])
    b.ve("tensor_scalar_mul", P["sdp"], P["sdp"], 64.0)
    b.ve("tensor_scalar_mul", P["sdp"], P["sdp"], P["rcvb"][:, 2:3])
    for a in range(16):
        b.ve("tensor_scalar_mul", P["hsu_ug"][:, a * 128:(a + 1) * 128],
             P["hcb"], P["suu"][:, a:a + 1])
    b.ve("memset", P["kp"], 0.0)
    b.ve("memset", P["vp"], 0.0)
    wkvh = Z0(8192, 4096).bitcast(BF16)
    wkvl = Z0(12288, 4096).bitcast(BF16)
    vh_ = b.se("copy", wkvh, Z0(0, 8192))
    b.W("ve", "pse", vh_)
    b.ve("tensor_tensor", wkvl, Z0(0, 8192), wkvh, op=ALU.subtract)
    b.barrier()

    b.phase("PA")
    xtm = lambda tt: Z0(tt * 2048, 2048)
    xngh = lambda c: Z0(4096 + c * 128, 128).bitcast(BF16)
    xngl = lambda c: Z0(6144 + c * 128, 128).bitcast(BF16)
    sqscr = Z1(0, 2048)
    stc = lambda tt, j: Z1(2048 + tt * 8 + j, 1)
    khg = lambda kt: Z1(2112 + kt * 256, 256)
    khg2 = lambda kt: Z1(2624 + kt * 256, 256)
    vhg = lambda kt: Z1(3136 + kt * 256, 256)
    rkA = Z1(3648, 256)
    rkB = Z1(3904, 256)
    pool_a = Z1(4416, 128)
    pool_b = Z1(4544, 64)
    ropet = Z1(5120, 256)

    def stats_norm(xt_ap, tt):
        s1, s2 = stc(tt, 0), stc(tt, 1)
        m, e2, msq, var, r = (stc(tt, 2), stc(tt, 3), stc(tt, 4), stc(tt, 5),
                              stc(tt, 6))
        vs = b.se("activation", sqscr, xt_ap, AF.Square, accum_out=s2)
        b.ve("reduce_sum", s1, xt_ap, axis=AX.X)
        b.ve("tensor_scalar_mul", m, s1, 1.0 / 2048.0)
        b.W("ve", "pse", vs)
        b.ve("tensor_scalar_mul", e2, s2, 1.0 / 2048.0)
        b.ve("tensor_scalar_mul", msq, m, m)
        vv = b.ve("tensor_tensor", var, e2, msq, op=ALU.subtract)
        b.W("se", "pve", vv)
        b.se("activation", r, var, AF.Ln, bias=P["epsb"])
        vs2 = b.se("activation", r, r, AF.Exp, scale=-0.5)
        b.W("ve", "pse", vs2)
        return b.ve("tensor_scalar", xt_ap, xt_ap, m, r,
                    op0=ALU.subtract, op1=ALU.mult)

    def rope(dst, src, A_ap, B_ap, tmp, width):
        b.ve("stream_shuffle", tmp[:, 0:width], src, SHUF16)
        b.ve("tensor_tensor", dst, src, A_ap[:, 0:width], op=ALU.mult)
        b.ve("tensor_tensor", tmp[:, 0:width], tmp[:, 0:width], B_ap[:, 0:width],
             op=ALU.mult)
        return b.ve("tensor_tensor", dst, dst, tmp[:, 0:width], op=ALU.add)

    def pool64(src, dst_ap):
        seq = [(src, 256, pool_a), (pool_a, 128, pool_b), (pool_b, 64, pool_a),
               (pool_a, 32, pool_b), (pool_b, 16, pool_a), (pool_a, 8, pool_b)]
        for s_ap, w, d_ap in seq:
            v = s_ap[:, 0:w].rearrange("p (b w) -> p b w", b=4)
            hw = w // 8
            b.ve("tensor_tensor",
                 d_ap[:, 0:w // 2].rearrange("p (b w) -> p b w", b=4),
                 v[:, :, 0:hw], v[:, :, hw:2 * hw], op=ALU.add)
        return b.ve("tensor_scalar_mul", dst_ap, pool_b[:, 0:4], 1.0 / 64.0)

    conv_units = []
    for i in range(16):
        conv_units.append((wq_d[i], wq_bh[i], wq_bl[i], 1024))
    for i in range(16):
        conv_units.append((wo_d[i], wo_bh[i], wo_bl[i], 1024))
    for i in range(64):
        conv_units.append((wup_d[i], wup_bh[i], wup_bl[i], 2048))
        conv_units.append((wgate_d[i], wg_bh[i], wg_bl[i], 2048))
    for sw_ in range(4):
        for k_ in range(64):
            conv_units.append((wdown_d[sw_, k_], wd_bh[sw_, k_],
                               wd_bl[sw_, k_], 512))
    conv_idx = [0]
    ev_cstage = {}

    def conv_step(n_units):
        for _ in range(n_units):
            i = conv_idx[0]
            if i >= len(conv_units):
                return
            conv_idx[0] += 1
            src, dsth, dstl, w = conv_units[i]
            cs = i % 2
            stage = Z1(5376 + cs * 2048, w)
            outh = Z1(9472 + cs * 1024, w // 2).bitcast(BF16)
            outl = Z1(11520 + cs * 1024, w // 2).bitcast(BF16)
            b.W("sy", "pge", ev_cstage.get(cs, 0))
            b.dma("sy", stage, src, sem=f"s_pc{cs}")
            vst = b.cnt[f"s_pc{cs}"]
            b.W("se", f"s_pc{cs}", vst)
            b.W("se", f"s_ph{cs}", b.cnt.get(f"s_ph{cs}", 0))
            vhi = b.se("copy", outh, stage)
            b.W("ge", "pse", vhi)
            b.W("ge", f"s_pl{cs}", b.cnt.get(f"s_pl{cs}", 0))
            vlo = b.ge("tensor_tensor", outl, stage, outh, op=ALU.subtract)
            ev_cstage[cs] = vlo
            b.W("ge", "pge", vlo)
            b.dma("ge", dsth, outh, sem=f"s_ph{cs}")
            b.dma("ge", dstl, outl, sem=f"s_pl{cs}")

    ev_xtm_free = 0
    ev_rk_free = 0
    ev_xng = {}
    ev_tbank = {}
    ev_kvbank = 0
    for hg in range(NHG):
        conv_step(13)
        b.W("sy", "pve", ev_xtm_free)
        for tt in range(2):
            b.dma("sy", xtm(tt),
                  xf[hg * HG + tt * 128: hg * HG + (tt + 1) * 128, :],
                  sem="s_xa")
        v_x = b.cnt["s_xa"]
        b.W("sy", "pve", ev_rk_free)
        b.dma("sy", rkA, ropek_d[hg, 0], sem="s_rk")
        b.dma("sy", rkB, ropek_d[hg, 1], sem="s_rk")
        v_rk = b.cnt["s_rk"]

        b.W("se", "s_xa", v_x)
        b.W("ve", "s_xa", v_x)
        norm_ev = [stats_norm(xtm(tt), tt) for tt in range(2)]
        b.W("te", "pve", norm_ev[1])
        for c in range(16):
            bank = c % 2
            b.W("te", "pve", ev_tbank.get(bank, 0))
            for tt in range(2):
                b.tr(pb[bank][:, tt * 128:(tt + 1) * 128],
                     xtm(tt)[:, c * 128:(c + 1) * 128], P["idn"])
            vt = b.cnt["pte"]
            b.W("se", "pte", vt)
            vhi = b.se("copy", xngh(c), pb[bank][:, 0:256])
            b.W("ve", "pse", vhi)
            ev_xng[c] = b.ve("tensor_tensor", xngl(c), pb[bank][:, 0:256],
                             xngh(c), op=ALU.subtract)
            ev_tbank[bank] = ev_xng[c]
        ev_xtm_free = ev_xng[15]

        b.W("te", "pve", ev_xng[15])
        b.W("te", "pve", ev_kvbank)
        for c in range(16):
            for w in range(4):
                wslice = lambda wt, cc=c, ww=w: wt[:, (cc * 4 + ww) * 128:
                                                  (cc * 4 + ww + 1) * 128]
                b.mm(pb[4 + w][:, 0:256], wslice(wkvh), xngh(c),
                     start=(c == 0), stop=False, inc=False)
                b.mm(pb[4 + w][:, 0:256], wslice(wkvh), xngl(c),
                     start=False, stop=False, inc=False)
                b.mm(pb[4 + w][:, 0:256], wslice(wkvl), xngh(c),
                     start=False, stop=(c == 15),
                     inc=(c == 15 and w == 3))
        v_kv = b.cnt["pte"]
        b.W("ve", "pte", v_kv)
        b.W("ve", "s_rk", v_rk)
        for kt in range(2):
            b.ve("tensor_copy", khg(kt), pb[4 + kt][:, 0:256])
            b.ve("tensor_copy", vhg(kt), pb[6 + kt][:, 0:256])
        for kt in range(2):
            rope(khg2(kt), khg(kt), rkA, rkB, ropet, 256)
        ev_rk_free = b.cnt["pve"]
        if hg < REMOTE // HG:
            for kt in range(2):
                pool64(khg2(kt),
                       P["kp"][:, kt * NKEYP + hg * 4: kt * NKEYP + hg * 4 + 4])
                pool64(vhg(kt),
                       P["vp"][:, kt * NKEYP + hg * 4: kt * NKEYP + hg * 4 + 4])
        else:
            off = NBLK + (hg - REMOTE // HG) * HG
            for kt in range(2):
                b.ve("tensor_copy",
                     P["kp"][:, kt * NKEYP + off: kt * NKEYP + off + HG],
                     khg2(kt))
                b.ve("tensor_copy",
                     P["vp"][:, kt * NKEYP + off: kt * NKEYP + off + HG],
                     vhg(kt))
        ev_kvbank = b.cnt["pve"]
    conv_step(1000)

    if DEBUG:
        b.barrier()
        b.dma("ge", dbg["d_kp"], P["kp"])
        b.dma("ge", dbg["d_vp"], P["vp"])

    b.phase("PA2")
    b.barrier()
    ev_xtm_free = 0
    ev_tbank = {}
    ev_xnd = {}
    for og in range(NOG):
        b.W("sy", "pve", ev_xtm_free)
        for tt in range(2):
            b.dma("sy", xtm(tt),
                  xo[og * HG + tt * 128: og * HG + (tt + 1) * 128, :],
                  sem="s_xa")
        v_x = b.cnt["s_xa"]
        b.W("se", "s_xa", v_x)
        b.W("ve", "s_xa", v_x)
        norm_ev = [stats_norm(xtm(tt), tt) for tt in range(2)]
        b.W("te", "pve", norm_ev[1])
        b.W("te", "s_xw", 32 * 16 * og)
        for c in range(16):
            bank = c % 2
            b.W("te", "pve", ev_tbank.get(bank, 0))
            for tt in range(2):
                b.tr(pb[bank][:, tt * 128:(tt + 1) * 128],
                     xtm(tt)[:, c * 128:(c + 1) * 128], P["idn"])
            vt = b.cnt["pte"]
            b.W("se", "pte", vt)
            vhi = b.se("copy", xngh(c), pb[bank][:, 0:256])
            b.W("ve", "pse", vhi)
            vv = b.ve("tensor_tensor", xngl(c), pb[bank][:, 0:256],
                      xngh(c), op=ALU.subtract)
            ev_tbank[bank] = vv
            b.W("sy", "pve", vv)
            b.dma("sy", xn_dh[c, :, og * HG:(og + 1) * HG], xngh(c), sem="s_xw")
            b.dma("sy", xn_dl[c, :, og * HG:(og + 1) * HG], xngl(c), sem="s_xw")
        ev_xtm_free = ev_tbank[1]
    if DEBUG:
        b.barrier()
        b.dma("ge", dbg["d_xn"], xn_dh)

    b.phase("VPT")
    b.barrier()
    b.ve("tensor_copy", P["kps"][0:64, :], P["kp"][64:128, :])
    b.ve("tensor_copy", P["kps"][64:128, :], P["kp"][0:64, :])
    ev_vb = {}
    for kc in range(9):
        for kt in range(2):
            bank = (kc * 2 + kt) % 2
            b.W("te", "pve", ev_vb.get(bank, 0))
            vt = b.tr(pb[bank][:, 0:128],
                      P["vp"][:, kt * NKEYP + kc * 128: kt * NKEYP + (kc + 1) * 128],
                      P["idn"])
            b.W("ve", "pte", vt)
            ev_vb[bank] = b.ve("tensor_copy",
                               P["vpt"][:, kc * 256 + kt * 128:
                                        kc * 256 + (kt + 1) * 128],
                               pb[bank][:, 0:128])
    if DEBUG:
        b.barrier()
        b.dma("ge", dbg["d_vpt"], P["vpt"])

    b.phase("PB")
    b.barrier()
    qhat = lambda m: Z0(m * 1024, 1024)
    wqbh = lambda s: Z0(8192 + s * 512, 512).bitcast(BF16)
    wqbl = lambda s: Z0(10240 + s * 512, 512).bitcast(BF16)
    xnch = lambda s: Z0(12288 + s * 256, 256).bitcast(BF16)
    xncl = lambda s: Z0(13312 + s * 256, 256).bitcast(BF16)
    rqA = Z1(0, 1024)
    rqB = Z1(1024, 1024)
    ropeqt = Z1(3072, 1024)
    b.dma("sy", rqA, ropeq_d[0], sem="s_rq")
    b.dma("sy", rqB, ropeq_d[1], sem="s_rq")
    v_rq = b.cnt["s_rq"]
    ev_slot = {}
    ev_qbank = 0
    for half in range(2):
        for c in range(16):
            s = c % 4
            b.W("ge", "pte", ev_slot.get(s, 0))
            b.dma("ge", wqbh(s), wq_bh[c], sem=f"s_w{s}")
            b.dma("ge", wqbl(s), wq_bl[c], sem=f"s_w{s}")
            vw = b.cnt[f"s_w{s}"]
            b.W("sy", "pte", ev_slot.get(s, 0))
            b.dma("sy", xnch(s), xn_dh[c, :, half * 512:(half + 1) * 512],
                  sem=f"s_q{s}")
            b.dma("sy", xncl(s), xn_dl[c, :, half * 512:(half + 1) * 512],
                  sem=f"s_q{s}")
            vx = b.cnt[f"s_q{s}"]
            b.W("te", f"s_w{s}", vw)
            b.W("te", f"s_q{s}", vx)
            if c == 0:
                b.W("te", "pve", ev_qbank)
            for m in range(8):
                wh = wqbh(s)[:, m * 128:(m + 1) * 128]
                wl = wqbl(s)[:, m * 128:(m + 1) * 128]
                b.mm(pb[m][:, 0:512], wh, xnch(s),
                     start=(c == 0), stop=False, inc=False)
                b.mm(pb[m][:, 0:512], wh, xncl(s),
                     start=False, stop=False, inc=False)
                b.mm(pb[m][:, 0:512], wl, xnch(s),
                     start=False, stop=(c == 15), inc=(m == 7))
            ev_slot[s] = b.cnt["pte"]
        v_q = b.cnt["pte"]
        b.W("ve", "pte", v_q)
        for m in range(8):
            b.ve("tensor_scalar_mul",
                 qhat(m)[:, half * 512:(half + 1) * 512], pb[m][:, 0:512],
                 P["smv"][:, m:m + 1])
        ev_qbank = b.cnt["pve"]
    b.W("ve", "s_rq", v_rq)
    for m in range(8):
        rope(qhat(m), qhat(m), rqA, rqB, ropeqt, 1024)
    if DEBUG:
        b.barrier()
        b.dma("ge", dbg["d_qhat"], Z0(0, 8192))

    b.phase("PC")
    b.barrier()
    maskb = lambda s: Z0(8192 + s * NKEYP, NKEYP)
    probs = Z0(10496, NKEYP)
    probsn = Z0(11648, NKEYP)
    probst = Z0(12800, NKEYP)
    red = lambda j: Z0(13952 + j, 1)
    ev_mask = {}
    ev_sc = 0
    ev_pt = 0
    ev_av = 0
    ev_probsn_rd = 0
    ev_probst_rd = 0
    for qt in range(NQT):
        ms = qt % 2
        b.W("sy", "pve", ev_mask.get(ms, 0))
        b.dma("sy", maskb(ms), mask_d[qt], sem=f"s_m{ms}")
        v_m = b.cnt[f"s_m{ms}"]
        for h in range(H):
            g = h // 4
            hh = h // 2
            lhs = qhat(hh)[(h % 2) * 64:(h % 2) * 64 + 64,
                           qt * 128:(qt + 1) * 128]
            kp0 = (g // 2) * NKEYP
            qb = (h % 2) * 64
            ksrc = P["kp"] if (h % 2) == (g % 2) else P["kps"]
            krows = ksrc[qb:qb + 64, :]
            b.W("te", "pve", ev_sc)
            b.mm(pb[0][:, 0:512], lhs, krows[:, kp0:kp0 + 512], inc=False)
            b.mm(pb[1][:, 0:512], lhs, krows[:, kp0 + 512:kp0 + 1024],
                 inc=False)
            v_s = b.mm(pb[2][:, 0:128], lhs, krows[:, kp0 + 1024:kp0 + 1152])
            b.W("ve", "pte", v_s)
            b.W("ve", f"s_m{ms}", v_m)
            b.ve("tensor_tensor", probs[:, 0:512], pb[0][:, 0:512],
                 maskb(ms)[:, 0:512], op=ALU.min)
            b.ve("tensor_tensor", probs[:, 512:1024], pb[1][:, 0:512],
                 maskb(ms)[:, 512:1024], op=ALU.min)
            ev_sc = b.ve("tensor_tensor", probs[:, 1024:1152], pb[2][:, 0:128],
                         maskb(ms)[:, 1024:1152], op=ALU.min)
            vred = b.ve("reduce_max", red(0), probs, axis=AX.X, negate=True)
            b.W("se", "pve", vred)
            vexp = b.se("activation", probs, probs, AF.Exp, bias=red(0),
                        accum_out=red(1))
            b.W("ve", "pse", vexp)
            b.ve("reciprocal", red(2), red(1))
            b.W("ve", "pte", ev_probsn_rd)
            vpn = b.ve("tensor_scalar_mul", probsn, probs, red(2))
            b.W("te", "pve", vpn)
            b.W("te", "pve", ev_pt)
            for kc in range(9):
                b.tr(pb[3 + kc // 4][:, (kc % 4) * 128:(kc % 4) * 128 + 128],
                     probsn[:, kc * 128:(kc + 1) * 128], P["idn"])
            ev_probsn_rd = b.cnt["pte"]
            v_t = b.cnt["pte"]
            b.W("ve", "pte", v_t)
            b.W("ve", "pte", ev_probst_rd)
            b.ve("tensor_copy", probst[:, 0:512], pb[3][:, 0:512])
            b.ve("tensor_copy", probst[:, 512:1024], pb[4][:, 0:512])
            ev_pt = b.ve("tensor_copy", probst[:, 1024:1152], pb[5][:, 0:128])
            b.W("te", "pve", ev_pt)
            b.W("te", "pve", ev_av)
            for kc in range(9):
                b.mm(pb[6][0:64, 0:128],
                     P["vpt"][:, kc * 256 + g * 64: kc * 256 + g * 64 + 64],
                     probst[:, kc * 128:(kc + 1) * 128],
                     start=(kc == 0), stop=(kc == 8), inc=(kc == 8))
            v_av = b.cnt["pte"]
            ev_probst_rd = v_av
            b.W("ve", "pte", v_av)
            ev_av = b.ve("tensor_copy",
                         P["attnv"][(h % 2) * 64:(h % 2) * 64 + 64,
                                    hh * TO + qt * 128: hh * TO + (qt + 1) * 128],
                         pb[6][0:64, 0:128])
        ev_mask[ms] = ev_sc
    if DEBUG:
        b.barrier()
        b.dma("ge", dbg["d_attnv"], P["attnv"])

    b.phase("PE1")
    b.barrier()
    x1h = Z0(0, 8192).bitcast(BF16)
    x1l = Z0(8192, 8192).bitcast(BF16)
    zba = lambda a: Z1(a * 256, 256)
    zbb = lambda a: Z1(4096 + a * 256, 256)
    xn1h = lambda s: Z1(8192 + s * 128, 128).bitcast(BF16)
    xn1l = lambda s: Z1(8704 + s * 128, 128).bitcast(BF16)
    ev_x1bank = {}
    ev_xnc1 = {}
    ev_x1c = 0
    for tc in range(4):
        for a in range(16):
            s = a % 4
            b.W("sy", "pte", ev_xnc1.get(s, 0))
            b.dma("sy", xn1h(s), xn_dh[a, :, tc * 256:(tc + 1) * 256],
                  sem=f"s_e{s}")
            b.dma("sy", xn1l(s), xn_dl[a, :, tc * 256:(tc + 1) * 256],
                  sem=f"s_e{s}")
            vx = b.cnt[f"s_e{s}"]
            b.W("te", f"s_e{s}", vx)
            bank = a % 4
            b.W("te", "pve", ev_x1bank.get(bank, 0))
            hs = P["hsu_ug"][:, a * 128:(a + 1) * 128]
            b.mm(pb[bank][:, 0:256], hs, xn1h(s), start=True, stop=False,
                 inc=False)
            v_m = b.mm(pb[bank][:, 0:256], hs, xn1l(s), start=False, stop=True)
            ev_xnc1[s] = v_m
            b.W("ve", "pte", v_m)
            ev_x1bank[bank] = b.ve("tensor_copy", zba(a), pb[bank][:, 0:256])
        for stage, hbit in enumerate([1, 2, 4, 8]):
            src = zba if stage % 2 == 0 else zbb
            for i in range(16):
                jlo, jhi = i & ~hbit, i | hbit
                op = ALU.add if (i & hbit) == 0 else ALU.subtract
                dst = (zbb if stage % 2 == 0 else zba)(i)
                b.ve("tensor_tensor", dst, src(jlo), src(jhi), op=op)
        v_bf = b.cnt["pve"]
        b.W("se", "pve", v_bf)
        for i in range(16):
            hslice = x1h[:, i * 1024 + tc * 256: i * 1024 + tc * 256 + 256]
            lslice = x1l[:, i * 1024 + tc * 256: i * 1024 + tc * 256 + 256]
            vhi = b.se("copy", hslice, zba(i))
            b.W("ve", "pse", vhi)
            ev_x1c = b.ve("tensor_tensor", lslice, zba(i), hslice,
                          op=ALU.subtract)
    if DEBUG:
        b.barrier()
        b.dma("ge", dbg["d_x1"], Z0(0, 16384))
    if DEBUG2:
        b.barrier()
        b.dma("ge", dbg2["e_x1"], Z0(0, 16384))

    b.phase("P1")
    b.barrier()
    wubh = lambda s: Z1(s * 1024, 1024).bitcast(BF16)
    wubl = lambda s: Z1(2048 + s * 1024, 1024).bitcast(BF16)
    wgbh = lambda s: Z1(4096 + s * 1024, 1024).bitcast(BF16)
    wgbl = lambda s: Z1(6144 + s * 1024, 1024).bitcast(BF16)
    hsud = lambda s: Z1(8192 + s * 64, 64).bitcast(BF16)
    gsil = lambda p: Z1(8448 + p * 512, 512)
    gtile = lambda p: Z1(9472 + p * 512, 512)
    gth = lambda p: Z1(10496 + p * 256, 256).bitcast(BF16)
    gtl = lambda p: Z1(11008 + p * 256, 256).bitcast(BF16)
    ytile = lambda s: Z1(11520 + s * 1024, 1024)
    ev_w = {}
    ev_hsu = {}
    ev_ysb = {}
    ev_chain = {}
    ev_yevac = {}
    pend = None
    for a in range(64):
        sl = a % 2
        b.W("sy", "pte", ev_w.get(sl, 0))
        b.dma("sy", wubh(sl), wup_bh[a], sem=f"s_u{sl}")
        b.dma("sy", wubl(sl), wup_bl[a], sem=f"s_u{sl}")
        b.dma("sy", wgbh(sl), wg_bh[a], sem=f"s_u{sl}")
        b.dma("sy", wgbl(sl), wg_bl[a], sem=f"s_u{sl}")
        vw = b.cnt[f"s_u{sl}"]
        b.W("ve", "pte", ev_hsu.get(sl, 0))
        vh = b.ve("tensor_scalar_mul", hsud(sl), P["hcb"], P["sud"][:, a:a + 1])
        b.W("te", f"s_u{sl}", vw)
        b.W("te", "pve", vh)
        for half in range(2):
            hi = a * 2 + half
            par = hi % 2
            bk = par * 3
            b.W("te", "pve", ev_chain.get(par, 0))
            for c in range(16):
                xh = x1h[:, c * 1024 + half * 512: c * 1024 + (half + 1) * 512]
                xl = x1l[:, c * 1024 + half * 512: c * 1024 + (half + 1) * 512]
                cs = slice(c * 128, (c + 1) * 128)
                b.mm(pb[bk][:, 0:512], wubh(sl)[:, cs], xh,
                     start=(c == 0), stop=False, inc=False)
                b.mm(pb[bk][:, 0:512], wubh(sl)[:, cs], xl,
                     start=False, stop=False, inc=False)
                b.mm(pb[bk][:, 0:512], wubl(sl)[:, cs], xh,
                     start=False, stop=(c == 15), inc=False)
                b.mm(pb[bk + 1][:, 0:512], wgbh(sl)[:, cs], xh,
                     start=(c == 0), stop=False, inc=False)
                b.mm(pb[bk + 1][:, 0:512], wgbh(sl)[:, cs], xl,
                     start=False, stop=False, inc=False)
                b.mm(pb[bk + 1][:, 0:512], wgbl(sl)[:, cs], xh,
                     start=False, stop=(c == 15),
                     inc=(c == 15))
            v_ug = b.cnt["pte"]
            if half == 1:
                ev_w[sl] = v_ug
            b.W("se", "pte", v_ug)
            b.se("activation", gsil(par), pb[bk + 1][:, 0:512], AF.Silu,
                 scale=P["sug"][:, 64 + a:64 + a + 1])
            v_sil = b.cnt["pse"]
            b.W("ve", "pse", v_sil)
            b.ve("scalar_tensor_tensor", gtile(par), pb[bk][:, 0:512],
                 P["sug"][:, a:a + 1], gsil(par), op0=ALU.mult, op1=ALU.mult)
            vgd = b.cnt["pve"]
            b.W("se", "pve", vgd)
            vghi = b.se("copy", gth(par), gtile(par))
            b.W("ve", "pse", vghi)
            ev_chain[par] = b.ve("tensor_tensor", gtl(par), gtile(par),
                                 gth(par), op=ALU.subtract)
            if pend is not None:
                pa_, ph_, pp_ = pend
                pbk = pp_ * 3
                b.W("te", "pve", ev_chain[pp_ + 0] if False else ev_chain.get(pp_, 0))
                b.W("te", "pve", ev_yevac.get(pp_, 0))
                b.mm(pb[pbk + 2][:, 0:512], hsud(pa_ % 2), gth(pp_),
                     start=True, stop=False, inc=False)
                v_y = b.mm(pb[pbk + 2][:, 0:512], hsud(pa_ % 2), gtl(pp_),
                           start=False, stop=True)
                if pa_ % 2 == 1 and ph_ == 1:
                    pass
                ev_hsu[pa_ % 2] = v_y
                b.W("ve", "pte", v_y)
                ys = pa_ % 2
                if ph_ == 0:
                    b.W("ve", f"s_y{ys}", ev_ysb.get(ys, 0))
                    b.ve("tensor_copy", ytile(ys)[:, 0:512], pb[pbk + 2][:, 0:512])
                else:
                    b.ve("tensor_copy", ytile(ys)[:, 512:1024],
                         pb[pbk + 2][:, 0:512])
                    vy2 = b.cnt["pve"]
                    b.W("ge", "pve", vy2)
                    b.W("sy", f"s_y{ys}", ev_ysb.get(ys, 0)) if False else None
                    b.dma("ge", y_d[pa_], ytile(ys), sem=f"s_y{ys}")
                    ev_ysb[ys] = b.cnt[f"s_y{ys}"]
                ev_yevac[pp_] = b.cnt["pve"]
            pend = (a, half, par)
    pa_, ph_, pp_ = pend
    pbk = pp_ * 3
    b.W("te", "pve", ev_chain.get(pp_, 0))
    b.W("te", "pve", ev_yevac.get(pp_, 0))
    b.mm(pb[pbk + 2][:, 0:512], hsud(pa_ % 2), gth(pp_),
         start=True, stop=False, inc=False)
    v_y = b.mm(pb[pbk + 2][:, 0:512], hsud(pa_ % 2), gtl(pp_),
               start=False, stop=True)
    b.W("ve", "pte", v_y)
    b.ve("tensor_copy", ytile(pa_ % 2)[:, 512:1024], pb[pbk + 2][:, 0:512])
    vy2 = b.cnt["pve"]
    b.W("ge", "pve", vy2)
    b.dma("ge", y_d[pa_], ytile(pa_ % 2), sem=f"s_y{pa_ % 2}")
    if DEBUG:
        b.barrier()
        b.dma("ge", dbg["d_y"], y_d)

    b.phase("P2")
    b.barrier()
    p2a = lambda a: Z0(a * 256, 256)
    p2b = lambda a: Z1(a * 256, 256)
    x2hb = lambda a: Z1(a * 128, 128).bitcast(BF16)
    x2lb = lambda a: Z1(8192 + a * 128, 128).bitcast(BF16)
    v_wr = 0
    v_cv_se = 0
    v_cv_ve = 0
    for tc in range(4):
        if tc > 0:
            b.W("sy", "pse", v_cv_se)
            b.W("sy", "pve", v_cv_ve)
            b.W("ge", "pse", v_cv_se)
            b.W("ge", "pve", v_cv_ve)
            b.W("sy", "dge", v_wr)
            b.W("ge", "dge", v_wr)
        for a in range(64):
            b.dma("sy" if a % 2 == 0 else "ge", p2a(a),
                  y_d[a, :, tc * 256:(tc + 1) * 256])
        v_ld_sy = b.cnt["dsy"]
        v_ld_ge = b.cnt["dge"]
        b.W("ve", "dsy", v_ld_sy)
        b.W("ve", "dge", v_ld_ge)
        b.W("ge", "dsy", v_ld_sy)
        b.W("ge", "dge", v_ld_ge)
        NV = 42
        for stage, hbit in enumerate([1, 2, 4, 8, 16, 32]):
            src = p2a if stage % 2 == 0 else p2b
            dst = p2b if stage % 2 == 0 else p2a
            if stage > 0:
                b.W("ve", "pge", b.cnt.get("pge", 0))
                b.W("ge", "pve", b.cnt.get("pve", 0))
            for i in range(64):
                jlo, jhi = i & ~hbit, i | hbit
                op = ALU.add if (i & hbit) == 0 else ALU.subtract
                eng = b.ve if i < NV else b.ge
                eng("tensor_tensor", dst(i), src(jlo), src(jhi), op=op)
        v_bf = b.cnt["pve"]
        v_bfg = b.cnt["pge"]
        b.W("se", "pge", v_bfg)
        b.W("se", "pve", v_bf)
        if tc > 0:
            b.W("se", "dge", v_wr)
        for a in range(64):
            vhi = b.se("copy", x2hb(a), p2a(a))
            b.W("ve", "pse", vhi)
            b.ve("tensor_tensor", x2lb(a), p2a(a), x2hb(a), op=ALU.subtract)
        v_cv_se = b.cnt["pse"]
        v_cv_ve = b.cnt["pve"]
        b.W("ge", "pse", v_cv_se)
        b.W("ge", "pve", v_cv_ve)
        b.W("ge", "pge", b.cnt.get("pge", 0))
        for a in range(64):
            b.dma("ge", x2_dh[a, :, tc * 256:(tc + 1) * 256], x2hb(a))
            b.dma("ge", x2_dl[a, :, tc * 256:(tc + 1) * 256], x2lb(a))
        v_wr = b.cnt["dge"]
    if DEBUG:
        b.barrier()
        b.dma("ge", dbg["d_x2"], x2_dh)

    b.phase("P3")
    b.barrier()
    acc = lambda m: Z0(m * 1024, 1024)
    wdbh = lambda s: Z1(s * 256, 256).bitcast(BF16)
    wdbl = lambda s: Z1(1024 + s * 256, 256).bitcast(BF16)
    x2ch = lambda s: Z1(2048 + s * 512, 512).bitcast(BF16)
    x2cl = lambda s: Z1(4096 + s * 512, 512).bitcast(BF16)
    ev_slot3 = {}
    ev_p3bank = 0
    for sw in range(4):
        for k in range(64):
            s = k % 4
            b.W("ge", "pte", ev_slot3.get(s, 0))
            b.dma("ge", wdbh(s), wd_bh[sw, k], sem=f"s_d{s}")
            b.dma("ge", wdbl(s), wd_bl[sw, k], sem=f"s_d{s}")
            vw = b.cnt[f"s_d{s}"]
            b.W("sy", "pte", ev_slot3.get(s, 0))
            b.dma("sy", x2ch(s), x2_dh[k], sem=f"s_x{s}")
            b.dma("sy", x2cl(s), x2_dl[k], sem=f"s_x{s}")
            vx = b.cnt[f"s_x{s}"]
            b.W("te", f"s_d{s}", vw)
            b.W("te", f"s_x{s}", vx)
            if k == 0:
                b.W("te", "pve", ev_p3bank)
            for j in range(4):
                for half in range(2):
                    js = slice(j * 128, (j + 1) * 128)
                    hs = slice(half * 512, (half + 1) * 512)
                    b.mm(pb[j * 2 + half][:, 0:512], wdbh(s)[:, js],
                         x2ch(s)[:, hs], start=(k == 0), stop=False, inc=False)
                    b.mm(pb[j * 2 + half][:, 0:512], wdbh(s)[:, js],
                         x2cl(s)[:, hs], start=False, stop=False, inc=False)
                    b.mm(pb[j * 2 + half][:, 0:512], wdbl(s)[:, js],
                         x2ch(s)[:, hs], start=False, stop=(k == 63),
                         inc=(j == 3 and half == 1))
            ev_slot3[s] = b.cnt["pte"]
        v_p3 = b.cnt["pte"]
        b.W("ve", "pte", v_p3)
        for j in range(4):
            for half in range(2):
                b.ve("tensor_scalar_mul",
                     acc(sw * 4 + j)[:, half * 512:(half + 1) * 512],
                     pb[j * 2 + half][:, 0:512],
                     P["sdp"][:, sw * 4 + j:sw * 4 + j + 1])
        ev_p3bank = b.cnt["pve"]

    b.phase("PD")
    b.barrier()
    wobh = lambda s: Z1(6144 + s * 512, 512).bitcast(BF16)
    wobl = lambda s: Z1(7168 + s * 512, 512).bitcast(BF16)
    avh = Z1(8192, 4096).bitcast(BF16)
    avl = Z1(12288, 4096).bitcast(BF16)
    vhi = b.se("copy", avh, P["attnv"])
    b.W("ve", "pse", vhi)
    v_avc = b.ve("tensor_tensor", avl, P["attnv"], avh, op=ALU.subtract)
    b.W("te", "pve", v_avc)
    ev_wo = {}
    ev_wobank = 0
    for m in range(16):
        s = m % 2
        b.W("ge", "pte", ev_wo.get(s, 0))
        b.dma("ge", wobh(s), wo_bh[m], sem=f"s_o{s}")
        b.dma("ge", wobl(s), wo_bl[m], sem=f"s_o{s}")
        vw = b.cnt[f"s_o{s}"]
        b.W("te", f"s_o{s}", vw)
        b.W("te", "pve", ev_wobank)
        for c in range(8):
            for half in range(2):
                cs = slice(c * 128, (c + 1) * 128)
                asl = slice(c * TO + half * 512, c * TO + (half + 1) * 512)
                b.mm(pb[half][:, 0:512], wobh(s)[:, cs], avh[:, asl],
                     start=(c == 0), stop=False, inc=False)
                b.mm(pb[half][:, 0:512], wobh(s)[:, cs], avl[:, asl],
                     start=False, stop=False, inc=False)
                b.mm(pb[half][:, 0:512], wobl(s)[:, cs], avh[:, asl],
                     start=False, stop=(c == 7),
                     inc=(c == 7 and half == 1))
        ev_wo[s] = b.cnt["pte"]
        v_wo = b.cnt["pte"]
        b.W("ve", "pte", v_wo)
        for half in range(2):
            b.ve("tensor_tensor", acc(m)[:, half * 512:(half + 1) * 512],
                 acc(m)[:, half * 512:(half + 1) * 512], pb[half][:, 0:512],
                 op=ALU.add)
        ev_wobank = b.cnt["pve"]
    if DEBUG:
        b.barrier()
        b.dma("ge", dbg["d_acc"], Z0(0, 16384))

    b.phase("PF")
    b.barrier()
    xob = lambda s: Z1(8192 + s * 2048, 2048)
    outb = lambda s: Z1(12288 + s * 2048, 2048)
    ev_xo = {}
    ev_outb = {}
    ev_fbank = {}
    for tt in range(NQT):
        s = tt % 2
        b.W("sy", "pve", ev_xo.get(s, 0))
        b.dma("sy", xob(s), xo[tt * 128:(tt + 1) * 128, :], sem=f"s_f{s}")
        v_x = b.cnt[f"s_f{s}"]
        for cg in range(4):
            bank = cg % 2
            b.W("te", "pve", ev_fbank.get(bank, 0))
            for cc in range(4):
                c = cg * 4 + cc
                b.tr(pb[bank][:, cc * 128:(cc + 1) * 128],
                     acc(c)[:, tt * 128:(tt + 1) * 128], P["idn"])
            v_t = b.cnt["pte"]
            b.W("ve", "pte", v_t)
            b.W("ve", f"s_f{s}", v_x)
            if cg == 0:
                b.W("ve", f"s_b{s}", ev_outb.get(s, 0))
            vv = b.ve("tensor_tensor", outb(s)[:, cg * 512:(cg + 1) * 512],
                      xob(s)[:, cg * 512:(cg + 1) * 512], pb[bank][:, 0:512],
                      op=ALU.add)
            ev_fbank[bank] = vv
        ev_xo[s] = b.cnt["pve"]
        b.W("ge", "pve", ev_xo[s])
        b.dma("ge", out_d[tt * 128:(tt + 1) * 128, :], outb(s), sem=f"s_b{s}")
        ev_outb[s] = b.cnt[f"s_b{s}"]
    if DEBUG2:
        b.barrier()
        b.dma("ge", dbg2["e_attnv"], P["attnv"])
        b.dma("ge", dbg2["e_y"], y_d)
        b.dma("ge", dbg2["e_x2"], x2_d)
        b.dma("ge", dbg2["e_xn"], xn_d)
    for s_, v_ in list(b.cnt.items()):
        if s_.startswith(("d", "s_")) and v_ > 0:
            b.W("ge", s_, v_)

    b.finish()
    return nc


def _host_prep(inputs):
    f32 = np.float32
    x = np.ascontiguousarray(np.asarray(inputs["L0_attn_x"], f32).reshape(T, D))
    WqS = np.asarray(inputs["L0_attn_Wq_sem"], f32)
    WkS = np.asarray(inputs["L0_attn_Wk_sem"], f32)
    WqG = np.asarray(inputs["L0_attn_Wq_geo"], f32)
    WkG = np.asarray(inputs["L0_attn_Wk_geo"], f32)
    Wv = np.asarray(inputs["L0_attn_Wv"], f32)
    Wo = np.asarray(inputs["L0_attn_Wo"], f32)
    ls = np.asarray(inputs["L0_attn_logit_scale"], f32)
    SUu = np.asarray(inputs["L0_ffn_SU_ug"], f32)
    SUd = np.asarray(inputs["L0_ffn_SU_dp"], f32)
    WSu = np.asarray(inputs["L0_ffn_Wscale_ug"], f32)
    WSd = np.asarray(inputs["L0_ffn_Wscale_dp"], f32)
    Wup = np.asarray(inputs["L0_ffn_W_up"], f32)
    Wg = np.asarray(inputs["L0_ffn_W_gate"], f32)
    Wd = np.asarray(inputs["L0_ffn_W_down"], f32)

    Wq_hat = np.empty((D, 1024), f32)
    for h in range(H):
        Wq_hat[:, h * 64:h * 64 + 32] = WqS[:, h * 32:(h + 1) * 32]
        Wq_hat[:, h * 64 + 32:h * 64 + 64] = WqG[:, h * 32:(h + 1) * 32]
    Wk_hat = np.empty((D, 256), f32)
    for g in range(HKV):
        Wk_hat[:, g * 64:g * 64 + 32] = WkS[:, g * 32:(g + 1) * 32]
        Wk_hat[:, g * 64 + 32:g * 64 + 64] = WkG[:, g * 32:(g + 1) * 32]
    Wkv_hat = np.concatenate([Wk_hat, Wv], axis=1)

    wq = np.ascontiguousarray(Wq_hat.reshape(16, 128, 1024))
    wkv = np.ascontiguousarray(
        Wkv_hat.reshape(16, 128, 4, 128).transpose(1, 0, 2, 3).reshape(128, 8192))
    wup = np.ascontiguousarray(
        Wup.T.reshape(16, 128, 64, 128).transpose(2, 1, 0, 3).reshape(64, 128, 2048))
    wgate = np.ascontiguousarray(
        Wg.T.reshape(16, 128, 64, 128).transpose(2, 1, 0, 3).reshape(64, 128, 2048))
    wdown = np.ascontiguousarray(
        Wd.T.reshape(64, 128, 4, 512).transpose(2, 0, 1, 3))
    wo = np.ascontiguousarray(
        Wo.reshape(8, 128, 16, 128).transpose(2, 1, 0, 3).reshape(16, 128, 1024))

    inv = (10000.0 ** -(np.arange(16) / 16.0)).astype(np.float64)
    ang = np.arange(T, dtype=np.float64)[:, None] * inv[None, :]
    cosT, sinT = np.cos(ang).astype(f32), np.sin(ang).astype(f32)
    A = np.zeros((128, T), f32)
    Btab = np.zeros((128, T), f32)
    for blk in range(2):
        r0 = blk * 64
        A[r0:r0 + 32, :] = 1.0
        for i in range(16):
            A[r0 + 32 + i, :] = cosT[:, i]
            A[r0 + 48 + i, :] = cosT[:, i]
            Btab[r0 + 32 + i, :] = -sinT[:, i]
            Btab[r0 + 48 + i, :] = sinT[:, i]
    ropek = np.ascontiguousarray(
        np.stack([A, Btab], 0).reshape(2, 128, NHG, HG).transpose(2, 0, 1, 3))

    k_pos = np.concatenate([np.arange(NBLK) * MBLK + (MBLK - 1),
                            np.arange(REMOTE, T),
                            np.full(NKEYP - NKEY, 2 * T)]).astype(np.int64)
    ispad = np.arange(NKEYP) >= NKEY

    in_maps = []
    for c in range(NCORE):
        q = np.arange(c * TO, (c + 1) * TO)
        vis = k_pos[None, :] <= q[:, None]
        cap = np.where(vis, POS,
                       np.where(ispad[None, :], NEGPAD, NEG)).astype(f32)
        im = {
            "xf": x, "xo": np.ascontiguousarray(x[c * TO:(c + 1) * TO]),
            "wq": wq, "wkv": wkv, "wup": wup, "wgate": wgate,
            "wdown": wdown, "wo": wo,
            "su_ug": np.ascontiguousarray(SUu.reshape(16, 128)),
            "su_dp": np.ascontiguousarray(SUd.reshape(64, 128)),
            "wsc_ug": np.ascontiguousarray(WSu.reshape(128, 128)),
            "wsc_dp": np.ascontiguousarray(WSd.reshape(16, 128)),
            "ls": np.ascontiguousarray(ls.reshape(1, 16)),
            "mask": np.ascontiguousarray(cap.reshape(NQT, 128, NKEYP)),
            "ropeq": np.ascontiguousarray(
                np.stack([A, Btab], 0)[:, :, c * TO:(c + 1) * TO]),
            "ropek": ropek,
        }
        in_maps.append(im)
    return in_maps


def kernel(**inputs):
    in_maps = _host_prep(inputs)
    nc = build_program()
    res = run_bass_kernel_spmd(nc, in_maps, core_ids=list(range(NCORE)))
    out = np.concatenate([r["out"] for r in res.results], axis=0)
    return out.reshape(1, T, D).astype(np.float32)
